# version 15 (final)
# baseline (speedup 1.0000x reference)
import contextlib
import numpy as np
import ml_dtypes

import concourse.bass as bass
import concourse.mybir as mybir
from concourse.bass_utils import run_bass_kernel_spmd

F32 = mybir.dt.float32
BF16 = mybir.dt.bfloat16
I32 = mybir.dt.int32
AF = mybir.ActivationFunctionType
ALU = mybir.AluOpType
AX = mybir.AxisListType
NPBF = ml_dtypes.bfloat16

D = 2048
S = 2048
NCORE = 8
HC = 16
NH = 32
HD = 64
ROT = 32
NPAIR = 16
NHG = 8
INTER = 5504
NIB = 43
EPS = 1e-5
GSL = 256
DSL = 688

DEBUG_TAPS = ()
_MAX_PHASE = 99
_P3_NHG = NHG
_P3_ATTN = True
_P3_GHOST = True
_P3_CC = True
_P3_DO_Q = True
_P3_DO_V = True
_P3_ROT = True
_P3_KMM = True
_P3_PERM = True
_AG_LOCAL = False


def _host_tables():
    t = np.arange(S, dtype=np.float64)
    i2 = np.arange(0, ROT, 2, dtype=np.float64)
    inv_freq = 1.0 / (10000.0 ** (i2 / ROT))
    freqs2 = np.repeat(t[:, None] * inv_freq[None, :], 2, axis=1)
    base = (i2 + 0.4 * ROT) / (1.4 * ROT)
    power = (t - S // 2) / 512.0
    sc = base[None, :] ** power[:, None]
    sc2 = np.concatenate([sc, sc], axis=1)
    cos = np.cos(freqs2)
    sin = np.sin(freqs2)
    sign = np.where(np.arange(ROT) % 2 == 0, -1.0, 1.0)[None, :]
    CS = (cos * sc2).T
    SS = (sign * sin * sc2).T
    CK = (cos / sc2).T
    SK = (sign * sin / sc2).T

    def pairblk(rot_tbl, passval):
        out = np.full((128, S), float(passval), np.float64)
        out[0:32] = rot_tbl
        out[64:96] = rot_tbl
        return out

    tb = {
        "CS2": pairblk(CS, 1.0), "SS2": pairblk(SS, 0.0),
        "CK2": pairblk(CK, 1.0), "SK2": pairblk(SK, 0.0),
    }
    tb = {k: np.ascontiguousarray(v, np.float32) for k, v in tb.items()}

    p128 = np.zeros((128, 128), np.float32)
    for bse in (0, 64):
        for j in range(ROT):
            p128[bse + (j ^ 1), bse + j] = 1.0
    tb["P128"] = p128.astype(NPBF)

    masks = np.zeros((128, 4 * 512), np.float32)
    for r in range(4):
        for p in range(128):
            masks[p, 512 * r + min(128 * r + p, 512): 512 * (r + 1)] = 1.0
    tb["MASKS"] = masks.astype(NPBF)

    g32 = (np.float32(1.0) - np.exp2(
        (-5.0 - np.arange(NH)).astype(np.float32))).astype(np.float32)
    g64 = g32.astype(np.float64)
    ii = np.arange(S, dtype=np.float64)
    gam2 = np.zeros((NPAIR, 2, S), np.float64)
    for P in range(NPAIR):
        gam2[P, 0] = g64[2 * P] ** ii
        gam2[P, 1] = g64[2 * P + 1] ** ii
    tb["GAM2"] = gam2.astype(np.float32)

    gmj = np.zeros((128, 512), np.float64)
    for J in range(16):
        for h in range(NH):
            gmj[:, 32 * J + h] = g64[h] ** (-(128.0 * J + np.arange(128)))
    tb["GMJ"] = gmj.astype(np.float32)

    sel4 = np.zeros((128, 16), np.float32)
    for tq in range(4):
        sel4[:, 4 * tq + tq] = 1.0
    tb["SEL4"] = sel4.astype(NPBF)

    e4 = np.zeros((4, 512), np.float32)
    for tq in range(4):
        e4[tq, 128 * tq: 128 * (tq + 1)] = 1.0
    tb["E4"] = e4

    blk2 = np.zeros((128, 2), np.float32)
    blk2[0:64, 0] = 1.0
    blk2[64:128, 1] = 1.0
    tb["BLK2"] = blk2

    c4 = np.zeros((128, 4), np.float32)
    c4[:, 0] = EPS
    c4[:, 1] = 1.0
    c4[:, 2] = -0.5
    c4[:, 3] = -1.0
    tb["C4"] = c4

    bc2 = np.zeros((2, 128), np.float32)
    bc2[0, 0:64] = 1.0
    bc2[1, 64:128] = 1.0
    tb["BC2"] = bc2
    return tb


ENGS = ("sync", "scalar", "vector", "tensor", "gpsimd")
Q_OF = {"sync": "qS", "scalar": "qA", "gpsimd": "qG"}


class Em:
    def __init__(self, nc, sems, owner, sem_alloc):
        self.nc = nc
        self.sems = sems
        self.owner = owner
        self.cnt = {k: 0 for k in sems}
        self.ops = {e: [] for e in ENGS}
        self.mark = {e: {} for e in ENGS}
        self.sem_alloc = sem_alloc
        self.stream_eng = {}
        self.dma = set()

    def _stream(self, name, eng):
        sname = "D_" + name
        if sname not in self.sems:
            self.sems[sname] = self.sem_alloc(sname)
            self.cnt[sname] = 0
            self.stream_eng[sname] = eng
            self.dma.add(sname)
        assert self.stream_eng[sname] == eng, (sname, eng)
        return sname

    OWN = {"vector": "sV", "scalar": "sSc", "gpsimd": "sPo"}

    def op(self, eng, build, waits=(), inc=None, amt=1):
        waits = list(waits)
        own = self.OWN.get(eng)
        if own is not None and self.cnt[own] > 0:
            waits.append((own, self.cnt[own]))
        ws = []
        for t in waits:
            if t is None:
                continue
            s, v = t
            if s == "sPE" and eng == "tensor":
                continue
            if self.mark[eng].get(s, 0) >= v:
                continue
            self.mark[eng][s] = v
            ws.append((self.sems[s], v))
        tok = None
        if inc is not None:
            self.cnt[inc] += amt
            tok = (inc, self.cnt[inc])
        sem_h = self.sems[inc] if inc is not None else None

        def thunk(e, ws=ws, build=build, sem_h=sem_h, amt=amt):
            for h, v in ws:
                e.wait_ge(h, v)
            inst = build(e)
            if sem_h is not None:
                inst.then_inc(sem_h, amt)

        self.ops[eng].append(thunk)
        return tok

    def dma_op(self, eng, out_ap, in_ap, waits=(), slow=False, stream=None):
        assert stream is not None
        sname = self._stream(stream, eng)
        kw = {"allow_slow_non_contiguous": True} if slow else {}
        return self.op(
            eng,
            lambda e, o=out_ap, i=in_ap, kw=kw: e.dma_start(out=o, in_=i, **kw),
            waits=waits, inc=sname, amt=16)

    CSEM = {"vector": "sV", "scalar": "sSc", "gpsimd": "sPo",
            "tensor": "sPE"}

    def _new_bar(self):
        name = f"bar{len([k for k in self.sems if k.startswith('bar')])}"
        self.sems[name] = self.sem_alloc(name)
        self.cnt[name] = 0
        return name

    def fence(self):
        bname = self._new_bar()
        for eng in ("sync", "scalar", "gpsimd"):
            ws = [(self.sems[s], self.cnt[s]) for s, e2 in
                  self.stream_eng.items() if e2 == eng and self.cnt[s] > 0]
            cs = self.CSEM.get(eng)
            if cs and self.cnt[cs] > 0:
                ws.append((self.sems[cs], self.cnt[cs]))

            def thunk(e, ws=ws, sb=self.sems[bname]):
                for h, v in ws:
                    e.wait_ge(h, v)
                e.nop().then_inc(sb, 1)

            self.ops[eng].append(thunk)
            self.cnt[bname] += 1
        guard = [(bname, 3)]
        for s in ("sPE", "sV", "sSc", "sPo"):
            if self.cnt[s] > 0:
                guard.append((s, self.cnt[s]))
        return guard

    def flush(self):
        nc = self.nc
        ops = dict(self.ops)
        with nc.Block() as block:
            @block.sync
            def _(eng):
                for f in ops["sync"]:
                    f(eng)

            @block.scalar
            def _(eng):
                for f in ops["scalar"]:
                    f(eng)

            @block.vector
            def _(eng):
                for f in ops["vector"]:
                    f(eng)

            @block.tensor
            def _(eng):
                for f in ops["tensor"]:
                    f(eng)

            @block.gpsimd
            def _(eng):
                for f in ops["gpsimd"]:
                    f(eng)
        self.ops = {e: [] for e in ENGS}

    def barrier(self):
        bname = self._new_bar()
        for eng in ENGS:
            ws = [(self.sems[s], self.cnt[s]) for s, e2 in
                  self.stream_eng.items() if e2 == eng and self.cnt[s] > 0]
            cs = self.CSEM.get(eng)
            if cs and self.cnt[cs] > 0:
                ws.append((self.sems[cs], self.cnt[cs]))

            def thunk(e, ws=ws, sb=self.sems[bname]):
                for h, v in ws:
                    e.wait_ge(h, v)
                e.nop().then_inc(sb, 1)
                e.wait_ge(sb, 5)

            self.ops[eng].append(thunk)
        self.cnt[bname] = 5
        for e in ENGS:
            self.mark[e][bname] = 5
            for sname, v in self.cnt.items():
                if not sname.startswith("bar"):
                    self.mark[e][sname] = max(self.mark[e].get(sname, 0), v)


class Banks:

    def __init__(self, pa, pb):
        self.pa, self.pb = pa, pb
        self.wr = [None] * 8
        self.rd = [[] for _ in range(8)]

    def ap(self, r, p0=0, p1=128, c0=0, c1=512):
        t = self.pa if r < 4 else self.pb
        base = 512 * (r % 4)
        return t[p0:p1, base + c0: base + c1]

    def write_waits(self, regions):
        ws = []
        for r in regions:
            ws += self.rd[r]
            if self.wr[r] is not None:
                ws.append(self.wr[r])
        return ws

    def set_written(self, regions, tok):
        for r in regions:
            self.wr[r] = tok
            self.rd[r] = []

    def read_waits(self, regions):
        return [self.wr[r] for r in regions]

    def set_read(self, regions, tok):
        for r in regions:
            self.rd[r].append(tok)


def build_program(max_phase=99, p3_nhg=None, p3_attn=None, p3_ghost=None,
                  p3_cc=None):
    p3_nhg = _P3_NHG if p3_nhg is None else p3_nhg
    p3_attn = _P3_ATTN if p3_attn is None else p3_attn
    p3_ghost = _P3_GHOST if p3_ghost is None else p3_ghost
    p3_cc = _P3_CC if p3_cc is None else p3_cc
    p3_do_q = _P3_DO_Q
    p3_do_v = _P3_DO_V
    p3_rot = _P3_ROT
    p3_kmm = _P3_KMM
    p3_perm = _P3_PERM
    nc = bass.Bass("TRN2", target_bir_lowering=False)
    tb = _host_tables()

    xT = nc.dram_tensor("xT", [D, S], F32, kind="ExternalInput")
    wq = nc.dram_tensor("wq", [D, D], F32, kind="ExternalInput")
    wk = nc.dram_tensor("wk", [D, D], F32, kind="ExternalInput")
    wv = nc.dram_tensor("wv", [D, D], F32, kind="ExternalInput")
    wg = nc.dram_tensor("wg", [D, D], F32, kind="ExternalInput")
    wo = nc.dram_tensor("wo", [D, D], F32, kind="ExternalInput")
    lut = nc.dram_tensor("lut", [16], F32, kind="ExternalInput")
    gnw = nc.dram_tensor("gnw", [NH], F32, kind="ExternalInput")
    gnb = nc.dram_tensor("gnb", [NH], F32, kind="ExternalInput")
    gsl = nc.dram_tensor("gsl", [GSL, INTER], I32, kind="ExternalInput")
    usl = nc.dram_tensor("usl", [GSL, INTER], I32, kind="ExternalInput")
    dsl = nc.dram_tensor("dsl", [DSL, D], I32, kind="ExternalInput")
    outT = nc.dram_tensor("outT", [D, S], F32, kind="ExternalOutput")

    mish_dr = nc.dram_tensor("mish_dr", [D, S], BF16)
    y_dr = nc.dram_tensor("y_dr", [D, S], F32)
    z_dr = nc.dram_tensor("z_dr", [D, S], BF16)
    ao_dr = nc.dram_tensor("ao_dr", [D, S], F32)
    x2_dr = nc.dram_tensor("x2_dr", [D, S], F32)
    ffn_dr = nc.dram_tensor("ffn_dr", [D, S], F32)
    locg = nc.dram_tensor("locg", [GSL, INTER], BF16)
    locu = nc.dram_tensor("locu", [GSL, INTER], BF16)
    locd = nc.dram_tensor("locd", [DSL, D], BF16)
    ag_space = {} if _AG_LOCAL else {"addr_space": "Shared"}
    agg = nc.dram_tensor("agg", [D, INTER], BF16, **ag_space)
    agu = nc.dram_tensor("agu", [D, INTER], BF16, **ag_space)
    agd = nc.dram_tensor("agd", [INTER, D], BF16, **ag_space)

    inl = {k: nc.inline_tensor(v, name="c_" + k) for k, v in tb.items()}
    taps = {}
    tap_shapes = {"xn1": ([D, S], BF16), "y": ([D, S], F32),
                  "ao": ([D, S], F32), "x2": ([D, S], F32),
                  "mish": ([D, S], BF16), "wgate": ([D, INTER], BF16),
                  "ffn": ([D, S], F32)}
    for name in DEBUG_TAPS:
        sh, dt = tap_shapes[name]
        taps[name] = nc.dram_tensor("tap_" + name, sh, dt,
                                    kind="ExternalOutput")

    es = contextlib.ExitStack()
    sems = {}
    for sname in ("sPE", "sV", "sSc", "sPo", "sB", "cc"):
        sems[sname] = es.enter_context(nc.semaphore(sname))
    owner = {"sPE": "tensor", "sV": "vector", "sSc": "scalar", "sPo": "gpsimd"}

    def _sem_alloc(name):
        return es.enter_context(nc.semaphore(name))

    E = Em(nc, sems, owner, _sem_alloc)

    pa = es.enter_context(nc.psum_tensor("PA", [128, 2048], F32))
    pb = es.enter_context(nc.psum_tensor("PB", [128, 2048], F32))
    B = Banks(pa, pb)

    s0 = contextlib.ExitStack()

    def sb0(name, shape, dt):
        return s0.enter_context(nc.sbuf_tensor(name, shape, dt))

    r_s = sb0("r_s", [4, 512], F32)
    r_q = sb0("r_q", [4, 512], F32)
    r_t = sb0("r_t", [4, 512], F32)
    rA = {k: sb0(f"rA{k}", [4, 512], F32) for k in (1, 2, 3, 4)}
    rB = {k: sb0(f"rB{k}", [4, 512], F32) for k in (1, 2, 3, 4)}
    sel4 = sb0("sel4", [128, 16], BF16)
    e4 = sb0("e4", [4, 512], F32)
    blk2 = sb0("blk2", [128, 2], F32)
    bc2 = sb0("bc2", [2, 128], F32)
    lutbc = sb0("lutbc", [128, 16], F32)
    gnw2 = sb0("gnw2", [2, 16], F32)
    gnb2 = sb0("gnb2", [2, 16], F32)
    c4sb = sb0("c4sb", [128, 4], F32)
    for ci, cv in enumerate((EPS, 1.0, -0.5, -1.0)):
        nc.const_aps.register(F32, float(cv), c4sb[:, ci: ci + 1])
    TOK_C4 = E.dma_op("sync", c4sb[:], inl["C4"][:], stream="boot")
    sGH = contextlib.ExitStack()
    gidx = sGH.enter_context(nc.sbuf_tensor("gidx", [128, 1376], BF16))
    gpl = sGH.enter_context(nc.sbuf_tensor("gpl", [128, 1376], BF16))
    gacc = sGH.enter_context(nc.sbuf_tensor("gacc", [128, 1376], BF16))

    tok_c = []
    for t_sb, t_dr in ((sel4, inl["SEL4"]), (e4, inl["E4"]),
                       (blk2, inl["BLK2"]), (bc2, inl["BC2"])):
        tok_c.append(E.dma_op("sync", t_sb[:], t_dr[:], stream="boot"))
    tok_c.append(E.dma_op(
        "sync", lutbc[:], lut.ap().unsqueeze(0).partition_broadcast(128)[:, 0],
        stream="boot"))
    tok_c.append(E.dma_op(
        "sync", gnw2[:], gnw.ap().rearrange("(c two) -> two c", two=2),
        slow=True, stream="boot"))
    tok_c.append(E.dma_op(
        "sync", gnb2[:], gnb.ap().rearrange("(c two) -> two c", two=2),
        slow=True, stream="boot"))
    TOK_BOOT = ("D_boot", E.cnt["D_boot"])
    TOK_SEL4 = TOK_E4 = TOK_BLK2 = TOK_BC2 = TOK_LUT = TOK_GNW = TOK_GNB = TOK_BOOT
    TOK_C4 = TOK_BOOT

    def ln_stats(src_dr, row_a, row_b, bufs, tc_list=(0, 1, 2, 3), col0=0,
                 colw=2048, chunk_wait=None, war_x=None, spfx="x1",
                 chunk_toks=None):
        xb, xf, sq = bufs
        if war_x is None:
            war_x = [[], []]
        nsub = colw // 512
        wA = B.write_waits([0]) + B.write_waits([4])
        tpe = None
        mm_last = [None, None]
        for hc in range(HC):
            bi = hc % 2
            tl = E.dma_op(
                "sync", xb[bi][:, 0:colw],
                src_dr[128 * hc: 128 * (hc + 1), col0: col0 + colw],
                waits=war_x[bi] + ([chunk_wait] if chunk_wait else [])
                + ([chunk_toks[hc]] if chunk_toks else []),
                stream=f"{spfx}{bi}")
            tv = E.op("vector", lambda e, b=bi, w=colw, xb=xb, xf=xf:
                      e.tensor_copy(xf[b][:, 0:w], xb[b][:, 0:w]),
                      waits=[tl, mm_last[bi]], inc="sV")
            tsq = E.op("scalar", lambda e, b=bi, w=colw, xb=xb, sq=sq:
                       e.activation(sq[b][:, 0:w], xb[b][:, 0:w], AF.Square),
                       waits=[tl, mm_last[bi]], inc="sSc")
            war_x[bi] = [tv, tsq]
            for sub in range(nsub):
                tq = tc_list[sub]
                for (reg, srcb, tok_in) in ((0, xf, tv), (4, sq, tsq)):
                    first = (hc == 0 and sub == 0)
                    last = (hc == HC - 1 and sub == nsub - 1)
                    tpe = E.op("tensor", lambda e, reg=reg, b=bi, sr=srcb,
                               tq=tq, sub=sub, first=first, last=last:
                               e.matmul(
                                   B.ap(reg, 0, 4, 0, 512),
                                   sel4[:, 4 * tq: 4 * tq + 4],
                                   sr[b][:, 512 * sub: 512 * (sub + 1)],
                                   start=first, stop=last,
                                   skip_group_check=True),
                               waits=([tok_in, TOK_SEL4]
                                      + (wA if first else [])), inc="sPE")
            mm_last[bi] = tpe
        B.set_written([0], tpe)
        B.set_written([4], tpe)
        t1 = E.op("vector", lambda e: e.tensor_copy(r_s[:], B.ap(0, 0, 4)),
                  waits=[tpe], inc="sV")
        t2 = E.op("vector", lambda e: e.tensor_copy(r_q[:], B.ap(4, 0, 4)),
                  waits=[tpe], inc="sV")
        B.set_read([0], t1)
        B.set_read([4], t2)
        inv = 1.0 / 2048.0
        E.op("vector", lambda e, s=inv: e.tensor_scalar_mul(r_s[:], r_s[:], s),
             waits=[t1], inc="sV")
        E.op("vector", lambda e, s=inv: e.tensor_scalar_mul(r_q[:], r_q[:], s),
             waits=[t2], inc="sV")
        E.op("vector", lambda e: e.tensor_mul(r_t[:], r_s[:], r_s[:]),
             inc="sV")
        tvv = E.op("vector", lambda e: e.tensor_sub(r_q[:], r_q[:], r_t[:]),
                   inc="sV")
        tl1 = E.op("scalar", lambda e: e.activation(
            r_t[:], r_q[:], AF.Ln, bias=float(EPS)), waits=[tvv, TOK_C4],
            inc="sSc")
        tl2 = E.op("scalar", lambda e, ra=row_a: e.activation(
            ra[:], r_t[:], AF.Exp, scale=-0.5), waits=[tl1, TOK_C4],
            inc="sSc")
        t3 = E.op("vector", lambda e, ra=row_a, rb=row_b: e.tensor_mul(
            rb[:], r_s[:], ra[:]), waits=[tl2], inc="sV")
        t4 = E.op("vector", lambda e, rb=row_b: e.tensor_scalar_mul(
            rb[:], rb[:], -1.0), waits=[t3], inc="sV")
        return t4

    def bcast_rows(row_a, row_b, tok_rows, subs):
        for k, sub in enumerate(subs):
            for (reg, row) in ((k, row_a), (4 + k, row_b)):
                w = B.write_waits([reg])
                t = E.op("tensor", lambda e, reg=reg, row=row, sub=sub:
                         e.matmul(
                             B.ap(reg), e4[:, 128 * sub: 128 * (sub + 1)],
                             row[:], start=True, stop=True,
                             skip_group_check=True),
                         waits=w + [tok_rows, TOK_E4], inc="sPE")
                B.set_written([reg], t)


    def _finish():
        gfin = E.fence()
        E.op("gpsimd", lambda e: e.nop(), waits=gfin, inc="sPo")
        E.flush()
        return nc

    ghost_steps = []
    war_ghost = {"idx": None, "acc": None}

    def make_ghost_piece(src_dr, loc_dr, row0, prow, c0, cw):
        st = {}

        def s_load():
            st["tl"] = E.dma_op("gpsimd", gidx[0:prow, 0:cw],
                                src_dr[row0: row0 + prow, c0: c0 + cw],
                                waits=[war_ghost["idx"], war_ghost["acc"]],
                                stream="gi")

        def s_v0():
            st["tacc"] = E.op("vector", lambda e, pr=prow, w=cw:
                              e.tensor_scalar(
                                  gacc[0:pr, 0:w], gidx[0:pr, 0:w],
                                  0.0, lutbc[0:pr, 0:1],
                                  ALU.is_equal, ALU.mult),
                              waits=[st["tl"], TOK_LUT, war_ghost["acc"]],
                              inc="sV")

        def mk_v(v):
            def s_v():
                tp = E.op("vector", lambda e, v=v, pr=prow, w=cw:
                          e.tensor_scalar(
                              gpl[0:pr, 0:w], gidx[0:pr, 0:w],
                              float(v), lutbc[0:pr, v: v + 1],
                              ALU.is_equal, ALU.mult),
                          waits=[st["tl"]], inc="sV")
                st["tacc"] = E.op("vector", lambda e, pr=prow, w=cw:
                                  e.tensor_add(gacc[0:pr, 0:w],
                                               gacc[0:pr, 0:w],
                                               gpl[0:pr, 0:w]),
                                  waits=[tp], inc="sV")
                if v == 15:
                    war_ghost["idx"] = st["tacc"]
            return s_v

        def s_store():
            tst = E.dma_op("gpsimd",
                           loc_dr[row0: row0 + prow, c0: c0 + cw],
                           gacc[0:prow, 0:cw], waits=[st["tacc"]],
                           stream="gs")
            war_ghost["acc"] = tst

        ghost_steps.append(("L", s_load))
        ghost_steps.append(("C", s_v0))
        for v in range(1, 16):
            ghost_steps.append(("C", mk_v(v)))
        ghost_steps.append(("C", s_store))

    _SEGS = ((0, 1376), (1376, 1376), (2752, 1376), (4128, 1376))
    for ch in range(2):
        for c0, cw in _SEGS:
            make_ghost_piece(gsl, locg, 128 * ch, 128, c0, cw)
    for ch in range(2):
        for c0, cw in _SEGS:
            make_ghost_piece(usl, locu, 128 * ch, 128, c0, cw)
    n_gu_steps = len(ghost_steps)
    for ch in range(6):
        prow = 128 if ch < 5 else 48
        for c0, cw in ((0, 1376), (1376, 672)):
            make_ghost_piece(dsl, locd, 128 * ch, prow, c0, cw)
    ghost_emitted = [0]

    def drain_ghost(k):
        while k > 0 and ghost_emitted[0] < len(ghost_steps):
            kind, fn = ghost_steps[ghost_emitted[0]]
            fn()
            ghost_emitted[0] += 1
            if kind == "C":
                k -= 1
        if ghost_emitted[0] < len(ghost_steps):
            kind, fn = ghost_steps[ghost_emitted[0]]
            if kind == "L":
                fn()
                ghost_emitted[0] += 1

    def drain_until(n):
        while ghost_emitted[0] < min(n, len(ghost_steps)):
            kind, fn = ghost_steps[ghost_emitted[0]]
            fn()
            ghost_emitted[0] += 1


    s1 = contextlib.ExitStack()
    xn1 = s1.enter_context(nc.sbuf_tensor("xn1", [128, HC * 2048], BF16))
    sL1 = contextlib.ExitStack()
    xb1 = [sL1.enter_context(nc.sbuf_tensor(f"xb1_{i}", [128, 2048], F32))
           for i in range(2)]
    xf1 = [sL1.enter_context(nc.sbuf_tensor(f"xf1_{i}", [128, 2048], BF16))
           for i in range(2)]
    sq1 = [sL1.enter_context(nc.sbuf_tensor(f"sq1_{i}", [128, 2048], BF16))
           for i in range(2)]
    ub1 = sL1.enter_context(nc.sbuf_tensor("ub1", [128, 2048], F32))

    war_x = [[], []]
    tA1 = ln_stats(xT, rA[1], rB[1], (xb1, xf1, sq1), war_x=war_x)
    bcast_rows(rA[1], rB[1], tA1, (0, 1, 2, 3))
    tok_xn1 = {}
    for hc in range(HC):
        bi = hc % 2
        tl = E.dma_op("sync", xb1[bi][:], xT[128 * hc: 128 * (hc + 1), :],
                      waits=war_x[bi], stream=f"x1{bi}")
        tu = E.op("vector", lambda e, b=bi: e.tensor_mul(
            ub1[:], xb1[b][:], pa[:]),
            waits=[tl] + B.read_waits([0, 1, 2, 3]), inc="sV")
        war_x[bi] = [tu]
        tx = E.op("vector", lambda e, hc=hc: e.tensor_add(
            xn1[:, 2048 * hc: 2048 * (hc + 1)], ub1[:], pb[:]),
            waits=[tu] + B.read_waits([4, 5, 6, 7]), inc="sV")
        tok_xn1[hc] = tx
        B.set_read([0, 1, 2, 3], tx)
        B.set_read([4, 5, 6, 7], tx)
    if "xn1" in taps:
        for hc in range(HC):
            E.dma_op("scalar", taps["xn1"][128 * hc: 128 * (hc + 1), :],
                     xn1[:, 2048 * hc: 2048 * (hc + 1)], waits=[tok_xn1[hc]],
                     stream="tap")
    sL1.close()

    if max_phase <= 1:
        return _finish()

    s2a = contextlib.ExitStack()
    wgbuf = [s2a.enter_context(nc.sbuf_tensor(f"wgb{i}", [128, HC * 256],
                                              BF16)) for i in range(2)]
    g_s = s2a.enter_context(nc.sbuf_tensor("g_s", [128, 2048], F32))
    t_a = s2a.enter_context(nc.sbuf_tensor("t_a", [128, 2048], F32))
    t_b = s2a.enter_context(nc.sbuf_tensor("t_b", [128, 2048], F32))
    mbuf = [s2a.enter_context(nc.sbuf_tensor(f"mb{i}", [128, 2048], BF16))
            for i in range(2)]

    g1 = E.fence()
    war_wg = [None, None]
    war_mb = [None, None]
    tok_mish = {}
    for q in range(8):
        qi = q % 2
        tok_wg = E.dma_op(
            "gpsimd", wgbuf[qi][:, :].rearrange("p (c n) -> p c n", n=256),
            wg.ap()[:, 256 * q: 256 * (q + 1)].rearrange(
                "(c p) n -> p c n", p=128),
            waits=([war_wg[qi]] if war_wg[qi] else []) + g1,
            stream=f"wg{qi}")
        for obl in range(2):
            ob = 2 * q + obl
            slot = ob % 2
            regs = [0, 1, 2, 3] if slot == 0 else [4, 5, 6, 7]
            wfirst = B.write_waits(regs) + [tok_wg]
            tmm = None
            for hc in range(HC):
                for tq in range(4):
                    tmm = E.op("tensor", lambda e, hc=hc, tq=tq, qi=qi,
                               obl=obl, regs=regs: e.matmul(
                                   B.ap(regs[tq]),
                                   wgbuf[qi][:, 256 * hc + 128 * obl:
                                             256 * hc + 128 * (obl + 1)],
                                   xn1[:, 2048 * hc + 512 * tq:
                                       2048 * hc + 512 * (tq + 1)],
                                   start=(hc == 0), stop=(hc == HC - 1),
                                   skip_group_check=True),
                               waits=((wfirst if hc == 0 else [])
                                      + [tok_xn1[hc]]), inc="sPE")
            B.set_written(regs, tmm)
            war_wg[qi] = tmm
            psrc = pa if slot == 0 else pb
            t1_ = E.op("scalar", lambda e, p=psrc: e.copy(g_s[:], p[:]),
                       waits=[tmm] + g1, inc="sSc")
            t2_ = E.op("scalar", lambda e, p=psrc: e.activation(
                t_a[:], p[:], AF.Abs), waits=[tmm], inc="sSc")
            B.set_read(regs, t2_)
            t3_ = E.op("scalar", lambda e: e.activation(
                t_b[:], t_a[:], AF.Exp, scale=-1.0), waits=[t2_, TOK_C4],
                inc="sSc")
            t4_ = E.op("scalar", lambda e: e.activation(
                t_a[:], t_b[:], AF.Ln, bias=1.0), waits=[t3_, TOK_C4],
                inc="sSc")
            t5_ = E.op("vector", lambda e: e.tensor_scalar_max(
                t_b[:], g_s[:], 0.0), waits=[t1_, t4_] + g1, inc="sV")
            t6_ = E.op("vector", lambda e: e.tensor_add(
                t_a[:], t_a[:], t_b[:]), waits=[t4_, t5_], inc="sV")
            t7_ = E.op("scalar", lambda e: e.activation(
                t_b[:], t_a[:], AF.Tanh), waits=[t6_], inc="sSc")
            t8_ = E.op("vector", lambda e, m=mbuf[ob % 2]: e.tensor_mul(
                m[:], g_s[:], t_b[:]),
                waits=[t7_] + ([war_mb[ob % 2]] if war_mb[ob % 2] else []),
                inc="sV")
            war_mb[ob % 2] = E.dma_op(
                "scalar", mish_dr[128 * ob: 128 * (ob + 1), :],
                mbuf[ob % 2][:], waits=[t8_], stream=f"ms{ob % 2}")
            tok_mish[ob] = war_mb[ob % 2]
            if _P3_GHOST:
                drain_ghost(8)
            if "mish" in taps:
                E.dma_op("scalar", taps["mish"][128 * ob: 128 * (ob + 1), :],
                         mbuf[ob % 2][:], waits=[t8_], stream="tap")
    s2a_last = E.fence()
    s2a.close()

    if max_phase <= 2:
        return _finish()

    s2b = contextlib.ExitStack()

    def sb2(name, shape, dt):
        return s2b.enter_context(nc.sbuf_tensor(name, shape, dt))

    tblC = sb2("tblC", [128, 2048], F32)
    tblS = sb2("tblS", [128, 2048], F32)
    p128 = sb2("p128", [128, 128], BF16)
    masks = sb2("masks", [128, 4 * 512], BF16)
    gmj = sb2("gmj", [128, 512], F32)
    wq_sb = sb2("wq_sb", [128, HC * 256], BF16)
    wk_sb = sb2("wk_sb", [128, HC * 256], BF16)
    wv_sb = sb2("wv_sb", [128, HC * 256], BF16)
    q_t = sb2("q_t", [128, 2 * 2048], BF16)
    k_t = sb2("k_t", [128, 2 * 2048], BF16)
    v_t = sb2("v_t", [128, HC * 256], BF16)
    qab = sb2("qab", [128, 2048], BF16)
    t1h = sb2("t1h", [128, 1024], F32)
    t2h = sb2("t2h", [128, 1024], F32)
    s2t = sb2("s2t", [128, 4 * 512], BF16)
    ret_sb = sb2("ret_sb", [128, 2048], F32)
    gam_sb = sb2("gam_sb", [128, 2048], F32)
    stat8 = sb2("stat8", [128, 8], F32)
    st2 = sb2("st2", [128, 2], F32)
    hs22 = sb2("hs22", [2, 2], F32)
    m2 = sb2("m2", [2, 1], F32)
    v2s = sb2("v2s", [2, 1], F32)
    sA2 = sb2("sA2", [2, 1], F32)
    sB2 = sb2("sB2", [2, 1], F32)
    sacol = sb2("sacol", [128, 1], F32)
    bcol = sb2("bcol", [128, 1], F32)
    yc = [sb2(f"yc{i}", [128, 512], F32) for i in range(2)]
    mzb = [sb2(f"mzb{i}", [128, 512], BF16) for i in range(2)]
    zcb = [sb2(f"zcb{i}", [128, 512], BF16) for i in range(2)]

    g2 = s2a_last
    E.dma_op("sync", p128[:], inl["P128"][:], waits=g2, stream="b2")
    E.dma_op("sync", masks[:], inl["MASKS"][:], waits=g2, stream="b2")
    TOK_GMJ = E.dma_op("sync", gmj[:], inl["GMJ"][:], waits=g2, stream="b2")
    TOK_P128 = TOK_MASK = TOK_GMJ
    tok_tbl = [E.dma_op("sync", tblC[:], inl["CK2"][:], waits=g2,
                        stream="tbC"),
               E.dma_op("sync", tblS[:], inl["SK2"][:], waits=g2,
                        stream="tbS")]

    war_w = {"q": [], "k": [], "v": []}
    tok_w = {}
    WSB = {"q": wq_sb, "k": wk_sb, "v": wv_sb}
    WDR = {"q": wq, "k": wk, "v": wv}

    def load_whg(mat, hg):
        tok_w[mat] = E.dma_op(
            "gpsimd", WSB[mat][:, :].rearrange("p (c n) -> p c n", n=256),
            WDR[mat].ap()[:, 256 * hg: 256 * (hg + 1)].rearrange(
                "(c p) n -> p c n", p=128),
            waits=war_w[mat] + g2, stream="w" + mat)

    load_whg("k", 0)
    load_whg("q", 0)
    load_whg("v", 0)

    war_qt = []
    war_kt = []
    war_vt = []
    war_qab = []
    war_s2 = [None] * 4
    war_ret = [None, None, None, None]
    war_gamsb = None
    war_gpair = None
    war_yc = [None, None]
    war_mz = [None, None]
    war_zc = [None, None]
    tok_zrow = {}
    war_stats = []

    def emit_qkv_block(mat, blk, dst, war_dst, tbl_toks, rot=True):
        slot_regs = [0, 1, 2, 3] if blk == 0 else [4, 5, 6, 7]
        psrc = pa if blk == 0 else pb
        wfirst = B.write_waits(slot_regs) + [tok_w[mat]]
        tmm = None
        for hc in range(HC):
            for tq in range(4):
                tmm = E.op("tensor", lambda e, hc=hc, tq=tq, blk=blk,
                           regs=slot_regs, wsb=WSB[mat]: e.matmul(
                               B.ap(regs[tq]),
                               wsb[:, 256 * hc + 128 * blk:
                                   256 * hc + 128 * (blk + 1)],
                               xn1[:, 2048 * hc + 512 * tq:
                                   2048 * hc + 512 * (tq + 1)],
                               start=(hc == 0), stop=(hc == HC - 1),
                               skip_group_check=True),
                           waits=((wfirst if hc == 0 else [])
                                  + [tok_xn1[hc]]), inc="sPE")
        B.set_written(slot_regs, tmm)
        if not rot:
            tevc = E.op("scalar", lambda e, p=psrc, dst=dst, blk=blk:
                        e.activation(dst[:, 2048 * blk: 2048 * (blk + 1)],
                                     p[:], AF.Copy),
                        waits=[tmm] + war_dst + g2, inc="sSc")
            B.set_read(slot_regs, tevc)
            return [tevc, tevc], tmm
        tc0 = E.op("scalar", lambda e, p=psrc: e.copy(qab[:], p[:]),
                   waits=[tmm] + war_qab + g2, inc="sSc")
        B.set_read(slot_regs, tc0)
        out_toks = []
        for th in range(2):
            ta = E.op("vector", lambda e, p=psrc, th=th: e.tensor_mul(
                t1h[:], p[:, 1024 * th: 1024 * (th + 1)],
                tblC[:, 1024 * th: 1024 * (th + 1)]),
                waits=[tmm, tc0] + tbl_toks + g2, inc="sV")
            tqb = None
            if p3_perm:
                for tq in (2 * th, 2 * th + 1):
                    tqb = E.op("tensor", lambda e, tq=tq, regs=slot_regs:
                               e.matmul(
                                   B.ap(regs[tq]), p128[:],
                                   qab[:, 512 * tq: 512 * (tq + 1)],
                                   start=True, stop=True,
                                   skip_group_check=True),
                               waits=[tc0, ta, TOK_P128], inc="sPE")
                war_qab.append(tqb)
                B.set_written([slot_regs[2 * th],
                               slot_regs[2 * th + 1]], tqb)
            tb_ = E.op("vector", lambda e, p=psrc, th=th: e.tensor_mul(
                t2h[:], p[:, 1024 * th: 1024 * (th + 1)],
                tblS[:, 1024 * th: 1024 * (th + 1)]),
                waits=[tqb] + tbl_toks, inc="sV")
            B.set_read([slot_regs[2 * th], slot_regs[2 * th + 1]], tb_)
            to = E.op("vector", lambda e, th=th, dst=dst, blk=blk:
                      e.tensor_add(
                          dst[:, 2048 * blk + 1024 * th:
                              2048 * blk + 1024 * (th + 1)],
                          t1h[:], t2h[:]),
                      waits=[tb_] + war_dst, inc="sV")
            out_toks.append(to)
        return out_toks, tmm

    for hg in range(p3_nhg):
        kt_toks = []
        kmm_last = None
        for blk in (range(2) if p3_kmm else range(0)):
            ts_, kmm_last = emit_qkv_block("k", blk, k_t, war_kt, tok_tbl,
                                           rot=p3_rot)
            kt_toks += ts_
        war_kt = []
        war_w["k"] = [kmm_last] if kmm_last else []
        if not kt_toks:
            kt_toks = [None]
        if p3_ghost:
            drain_ghost(6)
        if hg < p3_nhg - 1:
            load_whg("k", hg + 1)
        tok_tbl = [E.dma_op("sync", tblC[:], inl["CS2"][:], waits=kt_toks,
                            stream="tbC"),
                   E.dma_op("sync", tblS[:], inl["SS2"][:], waits=kt_toks,
                            stream="tbS")]
        qt_toks = []
        qmm_last = None
        for blk in (range(2) if p3_do_q else range(0)):
            ts_, qmm_last = emit_qkv_block("q", blk, q_t, war_qt, tok_tbl,
                                           rot=p3_rot)
            qt_toks += ts_
        if not qt_toks:
            qt_toks = kt_toks
        war_qt = []
        war_w["q"] = [qmm_last] if qmm_last else []
        if p3_ghost:
            drain_ghost(6)
        if hg < p3_nhg - 1:
            load_whg("q", hg + 1)
        vt_toks = []
        vmm_last = None
        for tb_i in (range(HC) if p3_do_v else range(0)):
            reg = tb_i % 2
            w0 = B.write_waits([reg]) + [tok_w["v"]]
            tmm = None
            for hc in range(HC):
                tmm = E.op("tensor", lambda e, hc=hc, tb_i=tb_i, reg=reg:
                           e.matmul(
                               B.ap(reg, 0, 128, 0, 256),
                               xn1[:, 2048 * hc + 128 * tb_i:
                                   2048 * hc + 128 * (tb_i + 1)],
                               wv_sb[:, 256 * hc: 256 * (hc + 1)],
                               start=(hc == 0), stop=(hc == HC - 1),
                               skip_group_check=True),
                           waits=((w0 if hc == 0 else [])
                                  + [tok_xn1[hc]]), inc="sPE")
            vmm_last = tmm
            B.set_written([reg], tmm)
            tv = E.op("scalar", lambda e, tb_i=tb_i, reg=reg: e.activation(
                v_t[:, 256 * tb_i: 256 * (tb_i + 1)],
                B.ap(reg, 0, 128, 0, 256), AF.Copy),
                waits=[tmm] + war_vt, inc="sSc")
            B.set_read([reg], tv)
            vt_toks.append(tv)
            if p3_ghost:
                drain_ghost(1)
        war_vt = []
        war_w["v"] = [vmm_last] if vmm_last else []
        if not vt_toks:
            vt_toks = kt_toks
        if hg < p3_nhg - 1:
            load_whg("v", hg + 1)
            tok_tbl = [E.dma_op("sync", tblC[:], inl["CK2"][:],
                                waits=qt_toks, stream="tbC"),
                       E.dma_op("sync", tblS[:], inl["SK2"][:],
                                waits=qt_toks, stream="tbS")]

        last_qk = None
        last_sv = None
        for p01 in (range(2) if p3_attn else range(0)):
            P = 2 * hg + p01
            g_src = inl["GAM2"].ap()
            tga = E.dma_op(
                "sync", gam_sb[0:64, :],
                g_src[P, 0].unsqueeze(0).partition_broadcast(64)[:, 0],
                waits=[war_gamsb] if war_gamsb else [], stream="gp")
            tgb = E.dma_op(
                "sync", gam_sb[64:128, :],
                g_src[P, 1].unsqueeze(0).partition_broadcast(64)[:, 0],
                waits=[], stream="gp")
            gb_last = tgb
            tok_gam = gb_last
            tok_mz_pre = {}
            for sb_ in range(2):
                tok_mz_pre[sb_] = E.dma_op(
                    "sync", mzb[sb_][:],
                    mish_dr[128 * P: 128 * (P + 1),
                            512 * sb_: 512 * (sb_ + 1)],
                    waits=[tok_mish[P], war_mz[sb_]] if war_mz[sb_]
                    else [tok_mish[P]], stream=f"mz{sb_}")

            ret_toks = []
            sq_toks = []

            def finalize_ret(I, rregs, tsv_h):
                for hh in range(2):
                    B.set_written([rregs[hh]], tsv_h[hh])
                    tev = E.op("vector", lambda e, rreg=rregs[hh], I=I,
                               hh=hh: e.scalar_tensor_tensor(
                                   ret_sb[64 * hh: 64 * (hh + 1),
                                          512 * I: 512 * (I + 1)],
                                   B.ap(rreg, 64 * hh, 64 * (hh + 1)), 1.0,
                                   gam_sb[64 * hh: 64 * (hh + 1),
                                          512 * I: 512 * (I + 1)],
                                   ALU.bypass, ALU.mult,
                                   accum_out=stat8[64 * hh: 64 * (hh + 1),
                                                   I: I + 1]),
                               waits=[tsv_h[hh], tok_gam], inc="sV")
                    tsq = E.op("scalar", lambda e, I=I, hh=hh: e.activation(
                        qab[0:64, 0:512],
                        ret_sb[64 * hh: 64 * (hh + 1),
                               512 * I: 512 * (I + 1)], AF.Square,
                        accum_out=stat8[64 * hh: 64 * (hh + 1),
                                        4 + I: 5 + I]),
                        waits=[tev], inc="sSc")
                    B.set_read([rregs[hh]], tev)
                    war_ret[2 * hh + I % 2] = tev
                    ret_toks.append(tev)
                    sq_toks.append(tsq)

            sv_state = {}

            def emit_sv(st):
                nonlocal last_sv
                I, J, te_h, rregs, first, last = st
                tsv_h = sv_state.setdefault(I, [None, None])
                for hh in range(2):
                    slot = (2 * J + hh) % 4
                    lh = 2 * p01 + hh
                    w_ret = (B.write_waits([rregs[hh]])
                             + [war_ret[I % 2], war_ret[2 + I % 2]]
                             ) if first else []
                    tsv = E.op("tensor", lambda e, hh=hh, J=J, slot=slot,
                               lh=lh, rreg=rregs[hh], first=first,
                               last=last: e.matmul(
                                   B.ap(rreg, 64 * hh, 64 * (hh + 1)),
                                   v_t[:, 256 * J + 64 * lh:
                                       256 * J + 64 * (lh + 1)],
                                   s2t[:, 512 * slot: 512 * (slot + 1)],
                                   start=first, stop=last,
                                   tile_position=(0, 64 * hh),
                                   skip_group_check=True),
                               waits=(w_ret + [te_h[hh], vt_toks[-1]]),
                               inc="sPE")
                    last_sv = tsv
                    tsv_h[hh] = tsv
                    war_s2[slot] = tsv
                if last:
                    finalize_ret(I, rregs, tsv_h)

            steps = [(I, J) for I in range(4) for J in range(4 * I + 4)]
            pend = None
            for (I, J) in steps:
                rregs = [4 + (I % 2), 6 + (I % 2)]
                nJ = 4 * I + 4
                te_h = [None, None]
                tqk_h = [None, None]
                for hh in range(2):
                    sr = (2 * J + hh) % 4
                    wS = B.write_waits([sr])
                    tqk = E.op("tensor", lambda e, hh=hh, sr=sr, J=J,
                               I=I, p01=p01: e.matmul(
                                   B.ap(sr),
                                   k_t[64 * hh: 64 * (hh + 1),
                                       2048 * p01 + 128 * J:
                                       2048 * p01 + 128 * (J + 1)],
                                   q_t[64 * hh: 64 * (hh + 1),
                                       2048 * p01 + 512 * I:
                                       2048 * p01 + 512 * (I + 1)],
                                   start=True, stop=True,
                                   tile_position=(64 * hh, 0),
                                   skip_group_check=True),
                               waits=wS + [kt_toks[-1], qt_toks[-1]],
                               inc="sPE")
                    last_qk = tqk
                    tqk_h[hh] = tqk
                    B.set_written([sr], tqk)
                for hh in range(2):
                    sr = (2 * J + hh) % 4
                    slot = (2 * J + hh) % 4
                    h_g = 2 * P + hh
                    gcol = gmj[:, 32 * J + h_g: 32 * J + h_g + 1]
                    if I == J // 4:
                        te = E.op("vector", lambda e, sr=sr, slot=slot,
                                  gcol=gcol, r=J % 4:
                                  e.scalar_tensor_tensor(
                                      s2t[:, 512 * slot:
                                          512 * (slot + 1)],
                                      B.ap(sr), gcol,
                                      masks[:, 512 * r: 512 * (r + 1)],
                                      ALU.mult, ALU.mult),
                                  waits=[tqk_h[hh], TOK_GMJ, TOK_MASK,
                                         war_s2[slot]] + g2, inc="sV")
                    elif hh == 0:
                        te = E.op("scalar", lambda e, sr=sr, slot=slot,
                                  gcol=gcol: e.activation(
                                      s2t[:, 512 * slot:
                                          512 * (slot + 1)],
                                      B.ap(sr), AF.Copy, scale=gcol),
                                  waits=[tqk_h[hh], TOK_GMJ,
                                         war_s2[slot]] + g2, inc="sSc")
                    else:
                        te = E.op("vector", lambda e, sr=sr, slot=slot,
                                  gcol=gcol: e.tensor_scalar_mul(
                                      s2t[:, 512 * slot:
                                          512 * (slot + 1)],
                                      B.ap(sr), gcol),
                                  waits=[tqk_h[hh], TOK_GMJ,
                                         war_s2[slot]] + g2, inc="sV")
                    B.set_read([sr], te)
                    te_h[hh] = te
                if pend is not None:
                    emit_sv(pend)
                pend = (I, J, te_h, rregs, J == 0, J == nJ - 1)
            emit_sv(pend)
            war_gamsb = ret_toks[-1]
            ts1 = E.op("vector", lambda e: e.tensor_reduce(
                st2[:, 0:1], stat8[:, 0:4], AX.X, ALU.add),
                waits=ret_toks + sq_toks + war_stats, inc="sV")
            ts2 = E.op("vector", lambda e: e.tensor_reduce(
                st2[:, 1:2], stat8[:, 4:8], AX.X, ALU.add),
                waits=sq_toks, inc="sV")
            sreg = 0
            w0 = B.write_waits([sreg])
            tm2 = E.op("tensor", lambda e, sreg=sreg: e.matmul(
                B.ap(sreg, 0, 2, 0, 2), blk2[:], st2[:],
                start=True, stop=True, skip_group_check=True),
                waits=w0 + [ts1, ts2, TOK_BLK2], inc="sPE")
            B.set_written([sreg], tm2)
            th1 = E.op("vector", lambda e, sreg=sreg: e.tensor_copy(
                hs22[:], B.ap(sreg, 0, 2, 0, 2)), waits=[tm2], inc="sV")
            B.set_read([sreg], th1)
            inv = 1.0 / float(S * HD)
            E.op("vector", lambda e, s=inv: e.tensor_scalar_mul(
                m2[:], hs22[:, 0:1], s), waits=[th1], inc="sV")
            E.op("vector", lambda e, s=inv: e.tensor_scalar_mul(
                v2s[:], hs22[:, 1:2], s), waits=[th1], inc="sV")
            E.op("vector", lambda e: e.tensor_mul(sA2[:], m2[:], m2[:]),
                 inc="sV")
            tvr = E.op("vector", lambda e: e.tensor_sub(
                v2s[:], v2s[:], sA2[:]), inc="sV")
            tr1 = E.op("scalar", lambda e: e.activation(
                sB2[:], v2s[:], AF.Ln, bias=float(EPS)),
                waits=[tvr, TOK_C4], inc="sSc")
            tr2 = E.op("scalar", lambda e: e.activation(
                sA2[:], sB2[:], AF.Exp, scale=-0.5), waits=[tr1, TOK_C4],
                inc="sSc")
            tr3 = E.op("vector", lambda e, P=P: e.tensor_mul(
                sA2[:], sA2[:], gnw2[:, P: P + 1]),
                waits=[tr2, TOK_GNW], inc="sV")
            tr4 = E.op("vector", lambda e: e.tensor_mul(
                sB2[:], m2[:], sA2[:]), waits=[tr3], inc="sV")
            tr5 = E.op("vector", lambda e: e.tensor_scalar_mul(
                sB2[:], sB2[:], -1.0), waits=[tr4], inc="sV")
            tr6 = E.op("vector", lambda e, P=P: e.tensor_add(
                sB2[:], sB2[:], gnb2[:, P: P + 1]),
                waits=[tr5, TOK_GNB], inc="sV")
            w0 = B.write_waits([sreg])
            tmA = E.op("tensor", lambda e, sreg=sreg: e.matmul(
                B.ap(sreg, 0, 128, 0, 1), bc2[:], sA2[:],
                start=True, stop=True, skip_group_check=True),
                waits=w0 + [tr3], inc="sPE")
            tmB = E.op("tensor", lambda e, sreg=sreg: e.matmul(
                B.ap(sreg, 0, 128, 1, 2), bc2[:], sB2[:],
                start=True, stop=True, skip_group_check=True),
                waits=[tr6], inc="sPE")
            B.set_written([sreg], tmB)
            tcA = E.op("vector", lambda e, sreg=sreg: e.tensor_copy(
                sacol[:], B.ap(sreg, 0, 128, 0, 1)), waits=[tmB], inc="sV")
            tcB = E.op("vector", lambda e, sreg=sreg: e.tensor_copy(
                bcol[:], B.ap(sreg, 0, 128, 1, 2)), waits=[tmB], inc="sV")
            B.set_read([sreg], tcA)
            B.set_read([sreg], tcB)
            war_stats = [tm2, tmA, tmB, tr1, tr2]
            for sub in range(4):
                ybi = sub % 2
                if sub < 2:
                    tmz = tok_mz_pre[sub]
                else:
                    tmz = E.dma_op(
                        "sync", mzb[ybi][:],
                        mish_dr[128 * P: 128 * (P + 1),
                                512 * sub: 512 * (sub + 1)],
                        waits=[tok_mish[P], war_mz[ybi]], stream=f"mz{ybi}")
                ty = E.op("vector", lambda e, sub=sub, ybi=ybi:
                          e.tensor_scalar(
                              yc[ybi][:],
                              ret_sb[:, 512 * sub: 512 * (sub + 1)],
                              sacol[:], bcol[:], ALU.mult, ALU.add),
                          waits=[tcA, tcB, war_yc[ybi]], inc="sV")
                tz = E.op("vector", lambda e, ybi=ybi: e.tensor_mul(
                    zcb[ybi][:], yc[ybi][:], mzb[ybi][:]),
                    waits=[ty, tmz] + ([war_zc[ybi]] if war_zc[ybi]
                                       else []), inc="sV")
                war_yc[ybi] = tz
                war_mz[ybi] = tz
                war_zc[ybi] = E.dma_op(
                    "scalar", z_dr[128 * P: 128 * (P + 1),
                                   512 * sub: 512 * (sub + 1)],
                    zcb[ybi][:], waits=[tz], stream=f"ys{ybi}")
            tok_zrow[P] = [war_zc[0], war_zc[1]]
            if "y" in taps:
                for sub in range(4):
                    ybi = sub % 2
                    tyy = E.op("vector", lambda e, sub=sub, ybi=ybi:
                               e.tensor_scalar(
                                   yc[ybi][:],
                                   ret_sb[:, 512 * sub: 512 * (sub + 1)],
                                   sacol[:], bcol[:], ALU.mult, ALU.add),
                               waits=[tcA, tcB, war_yc[ybi]], inc="sV")
                    war_yc[ybi] = E.dma_op(
                        "scalar",
                        taps["y"][128 * P: 128 * (P + 1),
                                  512 * sub: 512 * (sub + 1)],
                        yc[ybi][:], waits=[tyy], stream="tap")
        if p3_attn:
            war_kt = [last_qk]
            war_qt = [last_qk]
            war_vt = [last_sv]
    if p3_ghost:
        drain_until(n_gu_steps)
    cc_toks = [None, None, None]
    if p3_cc:
        cc_toks = []
        gst_tok = ("D_gs", E.cnt.get("D_gs", 0))
        for loc, ag in ((locg, agg), (locu, agu)):
            t = E.op("gpsimd", lambda e, loc=loc, ag=ag: e.collective_compute(
                "AllGather", ALU.bypass,
                replica_groups=[list(range(NCORE))],
                ins=[loc[:]], outs=[ag[:]]),
                waits=[gst_tok], inc="cc")
            cc_toks.append(t)
        cc_toks.append(None)
    if "wgate" in taps:
        E.dma_op("scalar", taps["wgate"][:], agg[:], waits=[cc_toks[0]],
                 stream="tap")

    g3 = E.fence()
    E.flush()
    s2b.close()
    s1.close()

    if max_phase <= 3:
        return _finish()

    s3 = contextlib.ExitStack()
    z_t = s3.enter_context(nc.sbuf_tensor("z_t", [128, HC * 2048], BF16))
    wobuf = [s3.enter_context(nc.sbuf_tensor(f"wob{i}", [128, HC * 256],
                                             BF16)) for i in range(2)]
    aobuf = [s3.enter_context(nc.sbuf_tensor(f"aob{i}", [128, 2048], F32))
             for i in range(2)]

    tok_z = {}
    for fc in range(HC):
        bi = fc % 4
        prev = ([tok_z[fc - 4]] if fc - 4 in tok_z else []) + g3
        tok_z[fc] = E.dma_op(
            "sync" if bi < 2 else "scalar",
            z_t[:, 2048 * fc: 2048 * (fc + 1)],
            z_dr[128 * fc: 128 * (fc + 1), :],
            waits=prev + tok_zrow[fc], stream=f"zl{bi}")

    war_wo = [None, None]
    war_ao = [None, None]
    tok_aow = {}
    for q in range(8):
        qi = q % 2
        tok_wo = E.dma_op(
            "gpsimd", wobuf[qi][:, :].rearrange("p (c n) -> p c n", n=256),
            wo.ap()[:, 256 * q: 256 * (q + 1)].rearrange(
                "(c p) n -> p c n", p=128),
            waits=([war_wo[qi]] if war_wo[qi] else []) + g3,
            stream=f"wo{qi}")
        for obl in range(2):
            ob = 2 * q + obl
            slot = ob % 2
            regs = [0, 1, 2, 3] if slot == 0 else [4, 5, 6, 7]
            wfirst = B.write_waits(regs) + [tok_wo]
            tmm = None
            for fc in range(HC):
                for tq in range(4):
                    tmm = E.op("tensor", lambda e, fc=fc, tq=tq, qi=qi,
                               obl=obl, regs=regs: e.matmul(
                                   B.ap(regs[tq]),
                                   wobuf[qi][:, 256 * fc + 128 * obl:
                                             256 * fc + 128 * (obl + 1)],
                                   z_t[:, 2048 * fc + 512 * tq:
                                       2048 * fc + 512 * (tq + 1)],
                                   start=(fc == 0), stop=(fc == HC - 1),
                                   skip_group_check=True),
                               waits=((wfirst if fc == 0 else [])
                                      + [tok_z[fc]]), inc="sPE")
            B.set_written(regs, tmm)
            war_wo[qi] = tmm
            psrc = pa if slot == 0 else pb
            tev = E.op("scalar", lambda e, p=psrc, bi=ob % 2: e.copy(
                aobuf[bi][:], p[:]),
                waits=[tmm] + ([war_ao[ob % 2]] if war_ao[ob % 2] else []),
                inc="sSc")
            B.set_read(regs, tev)
            war_ao[ob % 2] = E.dma_op(
                "scalar", ao_dr[128 * ob: 128 * (ob + 1), :],
                aobuf[ob % 2][:], waits=[tev], stream=f"as{ob % 2}")
            tok_aow[ob] = war_ao[ob % 2]
            if p3_ghost:
                drain_ghost(8)
            if "ao" in taps:
                E.dma_op("scalar", taps["ao"][128 * ob: 128 * (ob + 1), :],
                         aobuf[ob % 2][:], waits=[tev], stream="tap")

    g4 = E.fence()
    E.flush()
    s3.close()

    if max_phase <= 4:
        return _finish()

    s4 = contextlib.ExitStack()
    xb4 = [s4.enter_context(nc.sbuf_tensor(f"xb4_{i}", [128, 2048], F32))
           for i in range(2)]
    xf4 = [s4.enter_context(nc.sbuf_tensor(f"xf4_{i}", [128, 2048], BF16))
           for i in range(2)]
    sq4 = [s4.enter_context(nc.sbuf_tensor(f"sq4_{i}", [128, 2048], BF16))
           for i in range(2)]
    ub4 = s4.enter_context(nc.sbuf_tensor("ub4", [128, 2048], F32))
    ub4b = s4.enter_context(nc.sbuf_tensor("ub4b", [128, 2048], F32))
    yb4 = [s4.enter_context(nc.sbuf_tensor(f"yb4_{i}", [128, 2048], F32))
           for i in range(2)]
    ob4 = [s4.enter_context(nc.sbuf_tensor(f"ob4_{i}", [128, 2048], F32))
           for i in range(2)]

    war_x = [g4, g4]
    tA2 = ln_stats(ao_dr, rA[2], rB[2], (xb4, xf4, sq4), war_x=war_x,
                   spfx="s4_", chunk_toks=tok_aow)
    bcast_rows(rA[2], rB[2], tA2, (0, 1, 2, 3))
    tok_x2w = {}
    war_yb = [[], []]
    war_ob = [None, None]
    for fc in range(HC):
        bi = fc % 2
        tl = E.dma_op("sync", xb4[bi][:],
                      ao_dr[128 * fc: 128 * (fc + 1), :], waits=war_x[bi],
                      stream=f"s4_{bi}")
        tl2 = E.dma_op("sync", yb4[bi][:],
                       xT[128 * fc: 128 * (fc + 1), :],
                       waits=war_yb[bi] + g4, stream=f"rt{bi}")
        tu = E.op("vector", lambda e, b=bi: e.tensor_mul(
            ub4[:], xb4[b][:], pa[:]),
            waits=[tl] + B.read_waits([0, 1, 2, 3]), inc="sV")
        war_x[bi] = [tu]
        tu2 = E.op("vector", lambda e: e.tensor_add(ub4b[:], ub4[:], pb[:]),
                   waits=[tu] + B.read_waits([4, 5, 6, 7]), inc="sV")
        tx2 = E.op("vector", lambda e, b=bi: e.tensor_add(
            ob4[b][:], ub4b[:], yb4[b][:]),
            waits=[tu2, tl2] + ([war_ob[bi]] if war_ob[bi] else []),
            inc="sV")
        war_yb[bi] = [tx2]
        B.set_read([0, 1, 2, 3], tu)
        B.set_read([4, 5, 6, 7], tu2)
        tok_x2w[fc] = E.dma_op("scalar", x2_dr[128 * fc: 128 * (fc + 1), :],
                               ob4[bi][:], waits=[tx2], stream=f"x2s{bi}")
        war_ob[bi] = tok_x2w[fc]
        if p3_ghost:
            drain_ghost(5)
        if "x2" in taps:
            E.dma_op("scalar", taps["x2"][128 * fc: 128 * (fc + 1), :],
                     ob4[bi][:], waits=[tx2], stream="tap")
    if p3_ghost:
        drain_ghost(len(ghost_steps))
    if p3_cc:
        gst2_tok = ("D_gs", E.cnt.get("D_gs", 0))
        cc_toks[2] = E.op("gpsimd", lambda e: e.collective_compute(
            "AllGather", ALU.bypass,
            replica_groups=[list(range(NCORE))],
            ins=[locd[:]], outs=[agd[:]]),
            waits=[gst2_tok], inc="cc")
    war_x = [[tok_x2w[HC - 2]], [tok_x2w[HC - 1]]]
    tA3 = ln_stats(x2_dr, rA[3], rB[3], (xb4, xf4, sq4), war_x=war_x,
                   spfx="s4_")

    g4b = E.fence()
    E.flush()
    s4.close()
    sGH.close()

    if max_phase <= 5:
        return _finish()

    s5 = contextlib.ExitStack()
    hmid = s5.enter_context(nc.sbuf_tensor("hmid", [128, NIB * 1024], BF16))
    silu_s = s5.enter_context(nc.sbuf_tensor("silu_s", [128, 1024], F32))
    x2b = [s5.enter_context(nc.sbuf_tensor(f"x2b{i}", [128, 1024], F32))
           for i in range(2)]
    fxb = [s5.enter_context(nc.sbuf_tensor(f"fxb{i}", [128, 1024], F32))
           for i in range(2)]
    u5a = s5.enter_context(nc.sbuf_tensor("u5a", [128, 1024], F32))
    u5b = s5.enter_context(nc.sbuf_tensor("u5b", [128, 1024], F32))
    xb5 = [s5.enter_context(nc.sbuf_tensor(f"xb5_{i}", [128, 1024], F32))
           for i in range(2)]
    fbuf = xb5
    xf5 = [s5.enter_context(nc.sbuf_tensor(f"xf5_{i}", [128, 1024], BF16))
           for i in range(2)]
    sq5 = [s5.enter_context(nc.sbuf_tensor(f"sq5_{i}", [128, 1024], BF16))
           for i in range(2)]

    g5 = g4b + E.fence()
    for tc in range(2):
        stc = contextlib.ExitStack()
        xn2 = stc.enter_context(
            nc.sbuf_tensor(f"xn2_{tc}", [128, HC * 1024], BF16))
        gbw = [stc.enter_context(nc.sbuf_tensor(
            f"gbw{tc}_{i}", [128, HC * 128], BF16)) for i in range(2)]
        ubw = [stc.enter_context(nc.sbuf_tensor(
            f"ubw{tc}_{i}", [128, HC * 128], BF16)) for i in range(2)]
        bcast_rows(rA[3], rB[3], tA3, (2 * tc, 2 * tc + 1))
        war_x2b = [[], []]
        tok_xn2 = {}
        for hc in range(HC):
            bi = hc % 2
            tl = E.dma_op("sync", x2b[bi][:],
                          x2_dr[128 * hc: 128 * (hc + 1),
                                1024 * tc: 1024 * (tc + 1)],
                          waits=war_x2b[bi] + [tok_x2w[hc]] + g5,
                          stream=f"x5{bi}")
            tu = E.op("vector", lambda e, b=bi: e.tensor_mul(
                u5a[:], x2b[b][:], pa[:, 0:1024]),
                waits=[tl] + B.read_waits([0, 1]) + g5, inc="sV")
            war_x2b[bi] = [tu]
            tok_xn2[hc] = E.op("vector", lambda e, hc=hc: e.tensor_add(
                xn2[:, 1024 * hc: 1024 * (hc + 1)], u5a[:],
                pb[:, 0:1024]), waits=[tu] + B.read_waits([4, 5]), inc="sV")
            B.set_read([0, 1], tu)
            B.set_read([4, 5], tok_xn2[hc])
        war_gb = [None, None]
        war_silu = None
        tok_hm = {}
        for ib in range(NIB):
            qi = ib % 2
            wq_ = [war_gb[qi]] if war_gb[qi] else []
            tg = E.dma_op(
                "sync", gbw[qi][:, :].rearrange("p (c n) -> p c n", n=128),
                agg.ap()[:, 128 * ib: 128 * (ib + 1)].rearrange(
                    "(c p) n -> p c n", p=128),
                waits=wq_ + [cc_toks[0]] + g5, stream=f"gw{qi}")
            tu_ = E.dma_op(
                "sync", ubw[qi][:, :].rearrange("p (c n) -> p c n", n=128),
                agu.ap()[:, 128 * ib: 128 * (ib + 1)].rearrange(
                    "(c p) n -> p c n", p=128),
                waits=wq_ + [cc_toks[1]] + g5, stream=f"uw{qi}")
            slot = ib % 2
            gregs = [0, 1] if slot == 0 else [4, 5]
            uregs = [2, 3] if slot == 0 else [6, 7]
            wfirst = B.write_waits(gregs + uregs) + [tg, tu_]
            tmm = None
            for hc in range(HC):
                for nh in range(2):
                    for (wsb, regs) in ((gbw[qi], gregs), (ubw[qi], uregs)):
                        tmm = E.op("tensor", lambda e, hc=hc, nh=nh,
                                   wsb=wsb, regs=regs: e.matmul(
                                       B.ap(regs[nh]),
                                       wsb[:, 128 * hc: 128 * (hc + 1)],
                                       xn2[:, 1024 * hc + 512 * nh:
                                           1024 * hc + 512 * (nh + 1)],
                                       start=(hc == 0),
                                       stop=(hc == HC - 1),
                                       skip_group_check=True),
                                   waits=((wfirst if hc == 0 else [])
                                          + [tok_xn2[hc]]), inc="sPE")
            B.set_written(gregs + uregs, tmm)
            war_gb[qi] = tmm
            pg = pa if slot == 0 else pb
            tsl = E.op("scalar", lambda e, p=pg: e.activation(
                silu_s[:], p[:, 0:1024], AF.Silu),
                waits=[tmm, war_silu], inc="sSc")
            B.set_read(gregs, tsl)
            thm = E.op("vector", lambda e, p=pg, ib=ib: e.tensor_mul(
                hmid[:, 1024 * ib: 1024 * (ib + 1)], silu_s[:],
                p[:, 1024:2048]), waits=[tsl], inc="sV")
            war_silu = thm
            B.set_read(uregs, thm)
            tok_hm[ib] = thm
        stc_mid = E.fence()
        stc.close()
        std = contextlib.ExitStack()
        dbw = [std.enter_context(nc.sbuf_tensor(
            f"dbw{tc}_{i}", [128, NIB * 256], BF16)) for i in range(2)]
        war_db = [None, None]
        war_fb = [None, None]
        tok_ffnw = {}
        for op2 in range(8):
            di = op2 % 2
            td = E.dma_op(
                "sync", dbw[di][:, :].rearrange("p (c n) -> p c n", n=256),
                agd.ap()[:, 256 * op2: 256 * (op2 + 1)].rearrange(
                    "(c p) n -> p c n", p=128),
                waits=([war_db[di]] if war_db[di] else []) + [cc_toks[2]]
                + stc_mid, stream=f"dw{di}")
            for ol in range(2):
                ob = 2 * op2 + ol
                slot = ob % 2
                regs = [0, 1] if slot == 0 else [4, 5]
                wfirst = B.write_waits(regs) + [td]
                tmm = None
                for ib in range(NIB):
                    for nh in range(2):
                        tmm = E.op("tensor", lambda e, ib=ib, nh=nh, ol=ol,
                                   di=di, regs=regs: e.matmul(
                                       B.ap(regs[nh]),
                                       dbw[di][:, 256 * ib + 128 * ol:
                                               256 * ib + 128 * (ol + 1)],
                                       hmid[:, 1024 * ib + 512 * nh:
                                            1024 * ib + 512 * (nh + 1)],
                                       start=(ib == 0),
                                       stop=(ib == NIB - 1),
                                       skip_group_check=True),
                                   waits=((wfirst if ib == 0 else [])
                                          + [tok_hm[ib]]), inc="sPE")
                B.set_written(regs, tmm)
                war_db[di] = tmm
                pd_ = pa if slot == 0 else pb
                tev = E.op("scalar", lambda e, p=pd_, bi=ob % 2: e.copy(
                    fbuf[bi][:], p[:, 0:1024]),
                    waits=[tmm] + ([war_fb[ob % 2]] if war_fb[ob % 2]
                                   else []), inc="sSc")
                B.set_read(regs, tev)
                war_fb[ob % 2] = E.dma_op(
                    "scalar", ffn_dr[128 * ob: 128 * (ob + 1),
                                     1024 * tc: 1024 * (tc + 1)],
                    fbuf[ob % 2][:], waits=[tev], stream=f"fs{ob % 2}")
                tok_ffnw[ob] = war_fb[ob % 2]
                if "ffn" in taps:
                    E.dma_op("scalar",
                             taps["ffn"][128 * ob: 128 * (ob + 1),
                                         1024 * tc: 1024 * (tc + 1)],
                             fbuf[ob % 2][:], waits=[tev], stream="tap")
        war_x5 = [[tok_ffnw[HC - 2]], [tok_ffnw[HC - 1]]]
        tA4 = ln_stats(ffn_dr, rA[4], rB[4], (xb5, xf5, sq5),
                       tc_list=(2 * tc, 2 * tc + 1), col0=1024 * tc,
                       colw=1024, war_x=war_x5, spfx="s5_",
                       chunk_toks=tok_ffnw)
        bcast_rows(rA[4], rB[4], tA4, (2 * tc, 2 * tc + 1))
        war_fxb = [list(war_x5[0]), list(war_x5[1])]
        war_o5 = [None, None]
        for ob in range(HC):
            bi = ob % 2
            tl = E.dma_op("sync", fxb[bi][:],
                          ffn_dr[128 * ob: 128 * (ob + 1),
                                 1024 * tc: 1024 * (tc + 1)],
                          waits=war_fxb[bi] + [tok_ffnw[ob]],
                          stream=f"fx{bi}")
            tl2 = E.dma_op("sync", x2b[bi][:],
                           x2_dr[128 * ob: 128 * (ob + 1),
                                 1024 * tc: 1024 * (tc + 1)],
                           waits=war_x2b[bi], stream=f"x5{bi}")
            tu = E.op("vector", lambda e, b=bi: e.tensor_mul(
                u5a[:], fxb[b][:], pa[:, 0:1024]),
                waits=[tl] + B.read_waits([0, 1]), inc="sV")
            tu2 = E.op("vector", lambda e: e.tensor_add(
                u5b[:], u5a[:], pb[:, 0:1024]),
                waits=[tu] + B.read_waits([4, 5]), inc="sV")
            tou = E.op("vector", lambda e, b=bi: e.tensor_add(
                fxb[b][:], u5b[:], x2b[b][:]),
                waits=[tu2, tl2], inc="sV")
            war_fxb[bi] = [tou]
            war_x2b[bi] = [tou]
            B.set_read([0, 1], tu)
            B.set_read([4, 5], tu2)
            tst = E.dma_op(
                "scalar", outT[128 * ob: 128 * (ob + 1),
                               1024 * tc: 1024 * (tc + 1)],
                fxb[bi][:], waits=[tou], stream=f"os{bi}")
            war_fxb[bi] = [tst]
            war_o5[bi] = tst
        E.barrier()
        E.flush()
        std.close()
    s5.close()

    return _finish()


def kernel(L0_attn_x, L0_attn_wqkv, L0_attn_wg, L0_attn_wo, L0_attn_gn_weight,
           L0_attn_gn_bias, L0_ffn_lut, L0_ffn_gate_idx, L0_ffn_up_idx,
           L0_ffn_down_idx):
    x = np.asarray(L0_attn_x, np.float32)
    wqkv = np.asarray(L0_attn_wqkv, np.float32)
    wg_ = np.asarray(L0_attn_wg, np.float32)
    wo_ = np.asarray(L0_attn_wo, np.float32)
    gnw_ = np.ascontiguousarray(np.asarray(L0_attn_gn_weight, np.float32))
    gnb_ = np.ascontiguousarray(np.asarray(L0_attn_gn_bias, np.float32))
    lut_ = np.ascontiguousarray(np.asarray(L0_ffn_lut, np.float32))
    gidx = np.asarray(L0_ffn_gate_idx, np.int32)
    uidx = np.asarray(L0_ffn_up_idx, np.int32)
    didx = np.asarray(L0_ffn_down_idx, np.int32)

    w3 = wqkv.reshape(NH, 3 * HD, D)
    wq_h = np.ascontiguousarray(w3[:, 0:HD].reshape(D, D).T)
    wk_h = np.ascontiguousarray(w3[:, HD: 2 * HD].reshape(D, D).T)
    wv_h = np.ascontiguousarray(w3[:, 2 * HD: 3 * HD].reshape(D, D).T)
    wg_h = np.ascontiguousarray(wg_.T)
    wo_h = np.ascontiguousarray(wo_.T)
    gT = np.ascontiguousarray(gidx.T)
    uT = np.ascontiguousarray(uidx.T)
    dT = np.ascontiguousarray(didx.T)

    nc = build_program(_MAX_PHASE)
    in_maps = []
    for c in range(NCORE):
        in_maps.append({
            "xT": np.ascontiguousarray(x[c].T),
            "wq": wq_h, "wk": wk_h, "wv": wv_h, "wg": wg_h, "wo": wo_h,
            "lut": lut_, "gnw": gnw_, "gnb": gnb_,
            "gsl": np.ascontiguousarray(gT[GSL * c: GSL * (c + 1)]),
            "usl": np.ascontiguousarray(uT[GSL * c: GSL * (c + 1)]),
            "dsl": np.ascontiguousarray(dT[DSL * c: DSL * (c + 1)]),
        })
    res = run_bass_kernel_spmd(nc, in_maps, core_ids=list(range(NCORE)))
    out = np.empty((NCORE, S, D), np.float32)
    for c in range(NCORE):
        out[c] = np.asarray(res.results[c]["outT"], np.float32).T
    return out


# version 16
# speedup vs baseline: 1.0037x; 1.0037x over previous
import contextlib
import numpy as np
import ml_dtypes

import concourse.bass as bass
import concourse.mybir as mybir
from concourse.bass_utils import run_bass_kernel_spmd

F32 = mybir.dt.float32
BF16 = mybir.dt.bfloat16
I32 = mybir.dt.int32
AF = mybir.ActivationFunctionType
ALU = mybir.AluOpType
AX = mybir.AxisListType
NPBF = ml_dtypes.bfloat16

D = 2048
S = 2048
NCORE = 8
HC = 16
NH = 32
HD = 64
ROT = 32
NPAIR = 16
NHG = 8
INTER = 5504
NIB = 43
EPS = 1e-5
GSL = 256
DSL = 688

DEBUG_TAPS = ()
_MAX_PHASE = 99
_P3_NHG = NHG
_P3_ATTN = True
_P3_GHOST = True
_P3_CC = True
_P3_DO_Q = True
_P3_DO_V = True
_P3_ROT = True
_P3_KMM = True
_P3_PERM = True
_AG_LOCAL = False


def _host_tables():
    t = np.arange(S, dtype=np.float64)
    i2 = np.arange(0, ROT, 2, dtype=np.float64)
    inv_freq = 1.0 / (10000.0 ** (i2 / ROT))
    freqs2 = np.repeat(t[:, None] * inv_freq[None, :], 2, axis=1)
    base = (i2 + 0.4 * ROT) / (1.4 * ROT)
    power = (t - S // 2) / 512.0
    sc = base[None, :] ** power[:, None]
    sc2 = np.concatenate([sc, sc], axis=1)
    cos = np.cos(freqs2)
    sin = np.sin(freqs2)
    sign = np.where(np.arange(ROT) % 2 == 0, -1.0, 1.0)[None, :]
    CS = (cos * sc2).T
    SS = (sign * sin * sc2).T
    CK = (cos / sc2).T
    SK = (sign * sin / sc2).T

    def pairblk(rot_tbl, passval):
        out = np.full((128, S), float(passval), np.float64)
        out[0:32] = rot_tbl
        out[64:96] = rot_tbl
        return out

    tb = {
        "CS2": pairblk(CS, 1.0), "SS2": pairblk(SS, 0.0),
        "CK2": pairblk(CK, 1.0), "SK2": pairblk(SK, 0.0),
    }
    tb = {k: np.ascontiguousarray(v, np.float32) for k, v in tb.items()}

    p128 = np.zeros((128, 128), np.float32)
    for bse in (0, 64):
        for j in range(ROT):
            p128[bse + (j ^ 1), bse + j] = 1.0
    tb["P128"] = p128.astype(NPBF)

    masks = np.zeros((128, 4 * 512), np.float32)
    for r in range(4):
        for p in range(128):
            masks[p, 512 * r + min(128 * r + p, 512): 512 * (r + 1)] = 1.0
    tb["MASKS"] = masks.astype(NPBF)

    g32 = (np.float32(1.0) - np.exp2(
        (-5.0 - np.arange(NH)).astype(np.float32))).astype(np.float32)
    g64 = g32.astype(np.float64)
    ii = np.arange(S, dtype=np.float64)
    gam2 = np.zeros((NPAIR, 2, S), np.float64)
    for P in range(NPAIR):
        gam2[P, 0] = g64[2 * P] ** ii
        gam2[P, 1] = g64[2 * P + 1] ** ii
    tb["GAM2"] = gam2.astype(np.float32)

    gmj = np.zeros((128, 512), np.float64)
    for J in range(16):
        for h in range(NH):
            gmj[:, 32 * J + h] = g64[h] ** (-(128.0 * J + np.arange(128)))
    tb["GMJ"] = gmj.astype(np.float32)

    sel4 = np.zeros((128, 16), np.float32)
    for tq in range(4):
        sel4[:, 4 * tq + tq] = 1.0
    tb["SEL4"] = sel4.astype(NPBF)

    e4 = np.zeros((4, 512), np.float32)
    for tq in range(4):
        e4[tq, 128 * tq: 128 * (tq + 1)] = 1.0
    tb["E4"] = e4

    blk2 = np.zeros((128, 2), np.float32)
    blk2[0:64, 0] = 1.0
    blk2[64:128, 1] = 1.0
    tb["BLK2"] = blk2

    c4 = np.zeros((128, 4), np.float32)
    c4[:, 0] = EPS
    c4[:, 1] = 1.0
    c4[:, 2] = -0.5
    c4[:, 3] = -1.0
    tb["C4"] = c4

    bc2 = np.zeros((2, 128), np.float32)
    bc2[0, 0:64] = 1.0
    bc2[1, 64:128] = 1.0
    tb["BC2"] = bc2
    return tb


ENGS = ("sync", "scalar", "vector", "tensor", "gpsimd")
Q_OF = {"sync": "qS", "scalar": "qA", "gpsimd": "qG"}


class Em:
    def __init__(self, nc, sems, owner, sem_alloc):
        self.nc = nc
        self.sems = sems
        self.owner = owner
        self.cnt = {k: 0 for k in sems}
        self.ops = {e: [] for e in ENGS}
        self.mark = {e: {} for e in ENGS}
        self.sem_alloc = sem_alloc
        self.stream_eng = {}
        self.dma = set()

    def _stream(self, name, eng):
        sname = "D_" + name
        if sname not in self.sems:
            self.sems[sname] = self.sem_alloc(sname)
            self.cnt[sname] = 0
            self.stream_eng[sname] = eng
            self.dma.add(sname)
        assert self.stream_eng[sname] == eng, (sname, eng)
        return sname

    OWN = {"vector": "sV", "scalar": "sSc", "gpsimd": "sPo"}

    def op(self, eng, build, waits=(), inc=None, amt=1):
        waits = list(waits)
        own = self.OWN.get(eng)
        if own is not None and self.cnt[own] > 0:
            waits.append((own, self.cnt[own]))
        ws = []
        for t in waits:
            if t is None:
                continue
            s, v = t
            if s == "sPE" and eng == "tensor":
                continue
            if self.mark[eng].get(s, 0) >= v:
                continue
            self.mark[eng][s] = v
            ws.append((self.sems[s], v))
        tok = None
        if inc is not None:
            self.cnt[inc] += amt
            tok = (inc, self.cnt[inc])
        sem_h = self.sems[inc] if inc is not None else None

        def thunk(e, ws=ws, build=build, sem_h=sem_h, amt=amt):
            for h, v in ws:
                e.wait_ge(h, v)
            inst = build(e)
            if sem_h is not None:
                inst.then_inc(sem_h, amt)

        self.ops[eng].append(thunk)
        return tok

    def dma_op(self, eng, out_ap, in_ap, waits=(), slow=False, stream=None):
        assert stream is not None
        sname = self._stream(stream, eng)
        kw = {"allow_slow_non_contiguous": True} if slow else {}
        return self.op(
            eng,
            lambda e, o=out_ap, i=in_ap, kw=kw: e.dma_start(out=o, in_=i, **kw),
            waits=waits, inc=sname, amt=16)

    CSEM = {"vector": "sV", "scalar": "sSc", "gpsimd": "sPo",
            "tensor": "sPE"}

    def _new_bar(self):
        name = f"bar{len([k for k in self.sems if k.startswith('bar')])}"
        self.sems[name] = self.sem_alloc(name)
        self.cnt[name] = 0
        return name

    def fence(self):
        bname = self._new_bar()
        for eng in ("sync", "scalar", "gpsimd"):
            ws = [(self.sems[s], self.cnt[s]) for s, e2 in
                  self.stream_eng.items() if e2 == eng and self.cnt[s] > 0]
            cs = self.CSEM.get(eng)
            if cs and self.cnt[cs] > 0:
                ws.append((self.sems[cs], self.cnt[cs]))

            def thunk(e, ws=ws, sb=self.sems[bname]):
                for h, v in ws:
                    e.wait_ge(h, v)
                e.nop().then_inc(sb, 1)

            self.ops[eng].append(thunk)
            self.cnt[bname] += 1
        guard = [(bname, 3)]
        for s in ("sPE", "sV", "sSc", "sPo"):
            if self.cnt[s] > 0:
                guard.append((s, self.cnt[s]))
        return guard

    def flush(self):
        nc = self.nc
        ops = dict(self.ops)
        with nc.Block() as block:
            @block.sync
            def _(eng):
                for f in ops["sync"]:
                    f(eng)

            @block.scalar
            def _(eng):
                for f in ops["scalar"]:
                    f(eng)

            @block.vector
            def _(eng):
                for f in ops["vector"]:
                    f(eng)

            @block.tensor
            def _(eng):
                for f in ops["tensor"]:
                    f(eng)

            @block.gpsimd
            def _(eng):
                for f in ops["gpsimd"]:
                    f(eng)
        self.ops = {e: [] for e in ENGS}

    def barrier(self):
        bname = self._new_bar()
        for eng in ENGS:
            ws = [(self.sems[s], self.cnt[s]) for s, e2 in
                  self.stream_eng.items() if e2 == eng and self.cnt[s] > 0]
            cs = self.CSEM.get(eng)
            if cs and self.cnt[cs] > 0:
                ws.append((self.sems[cs], self.cnt[cs]))

            def thunk(e, ws=ws, sb=self.sems[bname]):
                for h, v in ws:
                    e.wait_ge(h, v)
                e.nop().then_inc(sb, 1)
                e.wait_ge(sb, 5)

            self.ops[eng].append(thunk)
        self.cnt[bname] = 5
        for e in ENGS:
            self.mark[e][bname] = 5
            for sname, v in self.cnt.items():
                if not sname.startswith("bar"):
                    self.mark[e][sname] = max(self.mark[e].get(sname, 0), v)


class Banks:

    def __init__(self, pa, pb):
        self.pa, self.pb = pa, pb
        self.wr = [None] * 8
        self.rd = [[] for _ in range(8)]

    def ap(self, r, p0=0, p1=128, c0=0, c1=512):
        t = self.pa if r < 4 else self.pb
        base = 512 * (r % 4)
        return t[p0:p1, base + c0: base + c1]

    def write_waits(self, regions):
        ws = []
        for r in regions:
            ws += self.rd[r]
            if self.wr[r] is not None:
                ws.append(self.wr[r])
        return ws

    def set_written(self, regions, tok):
        for r in regions:
            self.wr[r] = tok
            self.rd[r] = []

    def read_waits(self, regions):
        return [self.wr[r] for r in regions]

    def set_read(self, regions, tok):
        for r in regions:
            self.rd[r].append(tok)


def build_program(max_phase=99, p3_nhg=None, p3_attn=None, p3_ghost=None,
                  p3_cc=None):
    p3_nhg = _P3_NHG if p3_nhg is None else p3_nhg
    p3_attn = _P3_ATTN if p3_attn is None else p3_attn
    p3_ghost = _P3_GHOST if p3_ghost is None else p3_ghost
    p3_cc = _P3_CC if p3_cc is None else p3_cc
    p3_do_q = _P3_DO_Q
    p3_do_v = _P3_DO_V
    p3_rot = _P3_ROT
    p3_kmm = _P3_KMM
    p3_perm = _P3_PERM
    nc = bass.Bass("TRN2", target_bir_lowering=False)
    tb = _host_tables()

    xT = nc.dram_tensor("xT", [D, S], F32, kind="ExternalInput")
    wq = nc.dram_tensor("wq", [D, D], F32, kind="ExternalInput")
    wk = nc.dram_tensor("wk", [D, D], F32, kind="ExternalInput")
    wv = nc.dram_tensor("wv", [D, D], F32, kind="ExternalInput")
    wg = nc.dram_tensor("wg", [D, D], F32, kind="ExternalInput")
    wo = nc.dram_tensor("wo", [D, D], F32, kind="ExternalInput")
    lut = nc.dram_tensor("lut", [16], F32, kind="ExternalInput")
    gnw = nc.dram_tensor("gnw", [NH], F32, kind="ExternalInput")
    gnb = nc.dram_tensor("gnb", [NH], F32, kind="ExternalInput")
    gsl = nc.dram_tensor("gsl", [GSL, INTER], I32, kind="ExternalInput")
    usl = nc.dram_tensor("usl", [GSL, INTER], I32, kind="ExternalInput")
    dsl = nc.dram_tensor("dsl", [DSL, D], I32, kind="ExternalInput")
    outT = nc.dram_tensor("outT", [D, S], F32, kind="ExternalOutput")

    mish_dr = nc.dram_tensor("mish_dr", [D, S], BF16)
    y_dr = nc.dram_tensor("y_dr", [D, S], F32)
    z_dr = nc.dram_tensor("z_dr", [D, S], BF16)
    ao_dr = nc.dram_tensor("ao_dr", [D, S], F32)
    x2_dr = nc.dram_tensor("x2_dr", [D, S], F32)
    ffn_dr = nc.dram_tensor("ffn_dr", [D, S], F32)
    locg = nc.dram_tensor("locg", [GSL, INTER], BF16)
    locu = nc.dram_tensor("locu", [GSL, INTER], BF16)
    locd = nc.dram_tensor("locd", [DSL, D], BF16)
    ag_space = {} if _AG_LOCAL else {"addr_space": "Shared"}
    agg = nc.dram_tensor("agg", [D, INTER], BF16, **ag_space)
    agu = nc.dram_tensor("agu", [D, INTER], BF16, **ag_space)
    agd = nc.dram_tensor("agd", [INTER, D], BF16, **ag_space)

    inl = {k: nc.inline_tensor(v, name="c_" + k) for k, v in tb.items()}
    taps = {}
    tap_shapes = {"xn1": ([D, S], BF16), "y": ([D, S], F32),
                  "ao": ([D, S], F32), "x2": ([D, S], F32),
                  "mish": ([D, S], BF16), "wgate": ([D, INTER], BF16),
                  "ffn": ([D, S], F32)}
    for name in DEBUG_TAPS:
        sh, dt = tap_shapes[name]
        taps[name] = nc.dram_tensor("tap_" + name, sh, dt,
                                    kind="ExternalOutput")

    es = contextlib.ExitStack()
    sems = {}
    for sname in ("sPE", "sV", "sSc", "sPo", "sB", "cc"):
        sems[sname] = es.enter_context(nc.semaphore(sname))
    owner = {"sPE": "tensor", "sV": "vector", "sSc": "scalar", "sPo": "gpsimd"}

    def _sem_alloc(name):
        return es.enter_context(nc.semaphore(name))

    E = Em(nc, sems, owner, _sem_alloc)

    pa = es.enter_context(nc.psum_tensor("PA", [128, 2048], F32))
    pb = es.enter_context(nc.psum_tensor("PB", [128, 2048], F32))
    B = Banks(pa, pb)

    s0 = contextlib.ExitStack()

    def sb0(name, shape, dt):
        return s0.enter_context(nc.sbuf_tensor(name, shape, dt))

    r_s = sb0("r_s", [4, 512], F32)
    r_q = sb0("r_q", [4, 512], F32)
    r_t = sb0("r_t", [4, 512], F32)
    rA = {k: sb0(f"rA{k}", [4, 512], F32) for k in (1, 2, 3, 4)}
    rB = {k: sb0(f"rB{k}", [4, 512], F32) for k in (1, 2, 3, 4)}
    sel4 = sb0("sel4", [128, 16], BF16)
    e4 = sb0("e4", [4, 512], F32)
    blk2 = sb0("blk2", [128, 2], F32)
    bc2 = sb0("bc2", [2, 128], F32)
    lutbc = sb0("lutbc", [128, 16], F32)
    gnw2 = sb0("gnw2", [2, 16], F32)
    gnb2 = sb0("gnb2", [2, 16], F32)
    c4sb = sb0("c4sb", [128, 4], F32)
    for ci, cv in enumerate((EPS, 1.0, -0.5, -1.0)):
        nc.const_aps.register(F32, float(cv), c4sb[:, ci: ci + 1])
    TOK_C4 = E.dma_op("sync", c4sb[:], inl["C4"][:], stream="boot")
    sGH = contextlib.ExitStack()
    gidx = sGH.enter_context(nc.sbuf_tensor("gidx", [128, 1376], BF16))
    gpl = sGH.enter_context(nc.sbuf_tensor("gpl", [128, 1376], BF16))
    gacc = sGH.enter_context(nc.sbuf_tensor("gacc", [128, 1376], BF16))

    tok_c = []
    for t_sb, t_dr in ((sel4, inl["SEL4"]), (e4, inl["E4"]),
                       (blk2, inl["BLK2"]), (bc2, inl["BC2"])):
        tok_c.append(E.dma_op("sync", t_sb[:], t_dr[:], stream="boot"))
    tok_c.append(E.dma_op(
        "sync", lutbc[:], lut.ap().unsqueeze(0).partition_broadcast(128)[:, 0],
        stream="boot"))
    tok_c.append(E.dma_op(
        "sync", gnw2[:], gnw.ap().rearrange("(c two) -> two c", two=2),
        slow=True, stream="boot"))
    tok_c.append(E.dma_op(
        "sync", gnb2[:], gnb.ap().rearrange("(c two) -> two c", two=2),
        slow=True, stream="boot"))
    TOK_BOOT = ("D_boot", E.cnt["D_boot"])
    TOK_SEL4 = TOK_E4 = TOK_BLK2 = TOK_BC2 = TOK_LUT = TOK_GNW = TOK_GNB = TOK_BOOT
    TOK_C4 = TOK_BOOT

    def ln_stats(src_dr, row_a, row_b, bufs, tc_list=(0, 1, 2, 3), col0=0,
                 colw=2048, chunk_wait=None, war_x=None, spfx="x1",
                 chunk_toks=None):
        xb, xf, sq = bufs
        if war_x is None:
            war_x = [[], []]
        nsub = colw // 512
        wA = B.write_waits([0]) + B.write_waits([4])
        tpe = None
        mm_last = [None, None]
        for hc in range(HC):
            bi = hc % 2
            tl = E.dma_op(
                "sync", xb[bi][:, 0:colw],
                src_dr[128 * hc: 128 * (hc + 1), col0: col0 + colw],
                waits=war_x[bi] + ([chunk_wait] if chunk_wait else [])
                + ([chunk_toks[hc]] if chunk_toks else []),
                stream=f"{spfx}{bi}")
            tv = E.op("vector", lambda e, b=bi, w=colw, xb=xb, xf=xf:
                      e.tensor_copy(xf[b][:, 0:w], xb[b][:, 0:w]),
                      waits=[tl, mm_last[bi]], inc="sV")
            tsq = E.op("scalar", lambda e, b=bi, w=colw, xb=xb, sq=sq:
                       e.activation(sq[b][:, 0:w], xb[b][:, 0:w], AF.Square),
                       waits=[tl, mm_last[bi]], inc="sSc")
            war_x[bi] = [tv, tsq]
            for sub in range(nsub):
                tq = tc_list[sub]
                for (reg, srcb, tok_in) in ((0, xf, tv), (4, sq, tsq)):
                    first = (hc == 0 and sub == 0)
                    last = (hc == HC - 1 and sub == nsub - 1)
                    tpe = E.op("tensor", lambda e, reg=reg, b=bi, sr=srcb,
                               tq=tq, sub=sub, first=first, last=last:
                               e.matmul(
                                   B.ap(reg, 0, 4, 0, 512),
                                   sel4[:, 4 * tq: 4 * tq + 4],
                                   sr[b][:, 512 * sub: 512 * (sub + 1)],
                                   start=first, stop=last,
                                   skip_group_check=True),
                               waits=([tok_in, TOK_SEL4]
                                      + (wA if first else [])), inc="sPE")
            mm_last[bi] = tpe
        B.set_written([0], tpe)
        B.set_written([4], tpe)
        t1 = E.op("vector", lambda e: e.tensor_copy(r_s[:], B.ap(0, 0, 4)),
                  waits=[tpe], inc="sV")
        t2 = E.op("vector", lambda e: e.tensor_copy(r_q[:], B.ap(4, 0, 4)),
                  waits=[tpe], inc="sV")
        B.set_read([0], t1)
        B.set_read([4], t2)
        inv = 1.0 / 2048.0
        E.op("vector", lambda e, s=inv: e.tensor_scalar_mul(r_s[:], r_s[:], s),
             waits=[t1], inc="sV")
        E.op("vector", lambda e, s=inv: e.tensor_scalar_mul(r_q[:], r_q[:], s),
             waits=[t2], inc="sV")
        E.op("vector", lambda e: e.tensor_mul(r_t[:], r_s[:], r_s[:]),
             inc="sV")
        tvv = E.op("vector", lambda e: e.tensor_sub(r_q[:], r_q[:], r_t[:]),
                   inc="sV")
        tl1 = E.op("scalar", lambda e: e.activation(
            r_t[:], r_q[:], AF.Ln, bias=float(EPS)), waits=[tvv, TOK_C4],
            inc="sSc")
        tl2 = E.op("scalar", lambda e, ra=row_a: e.activation(
            ra[:], r_t[:], AF.Exp, scale=-0.5), waits=[tl1, TOK_C4],
            inc="sSc")
        t3 = E.op("vector", lambda e, ra=row_a, rb=row_b: e.tensor_mul(
            rb[:], r_s[:], ra[:]), waits=[tl2], inc="sV")
        t4 = E.op("vector", lambda e, rb=row_b: e.tensor_scalar_mul(
            rb[:], rb[:], -1.0), waits=[t3], inc="sV")
        return t4

    def bcast_rows(row_a, row_b, tok_rows, subs):
        for k, sub in enumerate(subs):
            for (reg, row) in ((k, row_a), (4 + k, row_b)):
                w = B.write_waits([reg])
                t = E.op("tensor", lambda e, reg=reg, row=row, sub=sub:
                         e.matmul(
                             B.ap(reg), e4[:, 128 * sub: 128 * (sub + 1)],
                             row[:], start=True, stop=True,
                             skip_group_check=True),
                         waits=w + [tok_rows, TOK_E4], inc="sPE")
                B.set_written([reg], t)


    def _finish():
        gfin = E.fence()
        E.op("gpsimd", lambda e: e.nop(), waits=gfin, inc="sPo")
        E.flush()
        return nc

    ghost_steps = []
    war_ghost = {"idx": None, "acc": None}

    def make_ghost_piece(src_dr, loc_dr, row0, prow, c0, cw):
        st = {}

        def s_load():
            st["tl"] = E.dma_op("gpsimd", gidx[0:prow, 0:cw],
                                src_dr[row0: row0 + prow, c0: c0 + cw],
                                waits=[war_ghost["idx"], war_ghost["acc"]],
                                stream="gi")

        def s_v0():
            st["tacc"] = E.op("vector", lambda e, pr=prow, w=cw:
                              e.tensor_scalar(
                                  gacc[0:pr, 0:w], gidx[0:pr, 0:w],
                                  0.0, lutbc[0:pr, 0:1],
                                  ALU.is_equal, ALU.mult),
                              waits=[st["tl"], TOK_LUT, war_ghost["acc"]],
                              inc="sV")

        def mk_v(v):
            def s_v():
                tp = E.op("vector", lambda e, v=v, pr=prow, w=cw:
                          e.tensor_scalar(
                              gpl[0:pr, 0:w], gidx[0:pr, 0:w],
                              float(v), lutbc[0:pr, v: v + 1],
                              ALU.is_equal, ALU.mult),
                          waits=[st["tl"]], inc="sV")
                st["tacc"] = E.op("vector", lambda e, pr=prow, w=cw:
                                  e.tensor_add(gacc[0:pr, 0:w],
                                               gacc[0:pr, 0:w],
                                               gpl[0:pr, 0:w]),
                                  waits=[tp], inc="sV")
                if v == 15:
                    war_ghost["idx"] = st["tacc"]
            return s_v

        def s_store():
            tst = E.dma_op("gpsimd",
                           loc_dr[row0: row0 + prow, c0: c0 + cw],
                           gacc[0:prow, 0:cw], waits=[st["tacc"]],
                           stream="gs")
            war_ghost["acc"] = tst

        ghost_steps.append(("L", s_load))
        ghost_steps.append(("C", s_v0))
        for v in range(1, 16):
            ghost_steps.append(("C", mk_v(v)))
        ghost_steps.append(("C", s_store))

    _SEGS = ((0, 1376), (1376, 1376), (2752, 1376), (4128, 1376))
    for ch in range(2):
        for c0, cw in _SEGS:
            make_ghost_piece(gsl, locg, 128 * ch, 128, c0, cw)
    for ch in range(2):
        for c0, cw in _SEGS:
            make_ghost_piece(usl, locu, 128 * ch, 128, c0, cw)
    n_gu_steps = len(ghost_steps)
    for ch in range(6):
        prow = 128 if ch < 5 else 48
        for c0, cw in ((0, 1376), (1376, 672)):
            make_ghost_piece(dsl, locd, 128 * ch, prow, c0, cw)
    ghost_emitted = [0]

    def drain_ghost(k):
        while k > 0 and ghost_emitted[0] < len(ghost_steps):
            kind, fn = ghost_steps[ghost_emitted[0]]
            fn()
            ghost_emitted[0] += 1
            if kind == "C":
                k -= 1
        if ghost_emitted[0] < len(ghost_steps):
            kind, fn = ghost_steps[ghost_emitted[0]]
            if kind == "L":
                fn()
                ghost_emitted[0] += 1

    def drain_until(n):
        while ghost_emitted[0] < min(n, len(ghost_steps)):
            kind, fn = ghost_steps[ghost_emitted[0]]
            fn()
            ghost_emitted[0] += 1


    s1 = contextlib.ExitStack()
    xn1 = s1.enter_context(nc.sbuf_tensor("xn1", [128, HC * 2048], BF16))
    s2a = contextlib.ExitStack()
    wgbuf = [s2a.enter_context(nc.sbuf_tensor(f"wgb{i}", [128, HC * 256],
                                              BF16)) for i in range(2)]
    g_s = s2a.enter_context(nc.sbuf_tensor("g_s", [128, 2048], F32))
    t_a = s2a.enter_context(nc.sbuf_tensor("t_a", [128, 2048], F32))
    t_b = s2a.enter_context(nc.sbuf_tensor("t_b", [128, 2048], F32))
    mbuf = [s2a.enter_context(nc.sbuf_tensor(f"mb{i}", [128, 2048], BF16))
            for i in range(2)]

    sL1 = contextlib.ExitStack()
    xb1 = [sL1.enter_context(nc.sbuf_tensor(f"xb1_{i}", [128, 2048], F32))
           for i in range(2)]
    xf1 = [sL1.enter_context(nc.sbuf_tensor(f"xf1_{i}", [128, 2048], BF16))
           for i in range(2)]
    sq1 = [sL1.enter_context(nc.sbuf_tensor(f"sq1_{i}", [128, 2048], BF16))
           for i in range(2)]
    ub1 = sL1.enter_context(nc.sbuf_tensor("ub1", [128, 2048], F32))

    war_x = [[], []]
    tA1 = ln_stats(xT, rA[1], rB[1], (xb1, xf1, sq1), war_x=war_x)
    bcast_rows(rA[1], rB[1], tA1, (0, 1, 2, 3))
    tok_xn1 = {}
    for hc in range(HC):
        bi = hc % 2
        tl = E.dma_op("sync", xb1[bi][:], xT[128 * hc: 128 * (hc + 1), :],
                      waits=war_x[bi], stream=f"x1{bi}")
        tu = E.op("vector", lambda e, b=bi: e.tensor_mul(
            ub1[:], xb1[b][:], pa[:]),
            waits=[tl] + B.read_waits([0, 1, 2, 3]), inc="sV")
        war_x[bi] = [tu]
        tx = E.op("vector", lambda e, hc=hc: e.tensor_add(
            xn1[:, 2048 * hc: 2048 * (hc + 1)], ub1[:], pb[:]),
            waits=[tu] + B.read_waits([4, 5, 6, 7]), inc="sV")
        tok_xn1[hc] = tx
        B.set_read([0, 1, 2, 3], tx)
        B.set_read([4, 5, 6, 7], tx)
    if "xn1" in taps:
        for hc in range(HC):
            E.dma_op("scalar", taps["xn1"][128 * hc: 128 * (hc + 1), :],
                     xn1[:, 2048 * hc: 2048 * (hc + 1)], waits=[tok_xn1[hc]],
                     stream="tap")
    sL1.close()

    if max_phase <= 1:
        return _finish()

    g1 = E.fence()
    war_wg = [None, None]
    war_mb = [None, None]
    tok_mish = {}
    for q in range(8):
        qi = q % 2
        tok_wg = E.dma_op(
            "gpsimd", wgbuf[qi][:, :].rearrange("p (c n) -> p c n", n=256),
            wg.ap()[:, 256 * q: 256 * (q + 1)].rearrange(
                "(c p) n -> p c n", p=128),
            waits=[war_wg[qi]] if war_wg[qi] else [],
            stream=f"wg{qi}")
        for obl in range(2):
            ob = 2 * q + obl
            slot = ob % 2
            regs = [0, 1, 2, 3] if slot == 0 else [4, 5, 6, 7]
            wfirst = B.write_waits(regs) + [tok_wg]
            tmm = None
            for hc in range(HC):
                for tq in range(4):
                    tmm = E.op("tensor", lambda e, hc=hc, tq=tq, qi=qi,
                               obl=obl, regs=regs: e.matmul(
                                   B.ap(regs[tq]),
                                   wgbuf[qi][:, 256 * hc + 128 * obl:
                                             256 * hc + 128 * (obl + 1)],
                                   xn1[:, 2048 * hc + 512 * tq:
                                       2048 * hc + 512 * (tq + 1)],
                                   start=(hc == 0), stop=(hc == HC - 1),
                                   skip_group_check=True),
                               waits=((wfirst if hc == 0 else [])
                                      + [tok_xn1[hc]]), inc="sPE")
            B.set_written(regs, tmm)
            war_wg[qi] = tmm
            psrc = pa if slot == 0 else pb
            t1_ = E.op("scalar", lambda e, p=psrc: e.copy(g_s[:], p[:]),
                       waits=[tmm] + g1, inc="sSc")
            t2_ = E.op("scalar", lambda e, p=psrc: e.activation(
                t_a[:], p[:], AF.Abs), waits=[tmm], inc="sSc")
            B.set_read(regs, t2_)
            t3_ = E.op("scalar", lambda e: e.activation(
                t_b[:], t_a[:], AF.Exp, scale=-1.0), waits=[t2_, TOK_C4],
                inc="sSc")
            t4_ = E.op("scalar", lambda e: e.activation(
                t_a[:], t_b[:], AF.Ln, bias=1.0), waits=[t3_, TOK_C4],
                inc="sSc")
            t5_ = E.op("vector", lambda e: e.tensor_scalar_max(
                t_b[:], g_s[:], 0.0), waits=[t1_, t4_] + g1, inc="sV")
            t6_ = E.op("vector", lambda e: e.tensor_add(
                t_a[:], t_a[:], t_b[:]), waits=[t4_, t5_], inc="sV")
            t7_ = E.op("scalar", lambda e: e.activation(
                t_b[:], t_a[:], AF.Tanh), waits=[t6_], inc="sSc")
            t8_ = E.op("vector", lambda e, m=mbuf[ob % 2]: e.tensor_mul(
                m[:], g_s[:], t_b[:]),
                waits=[t7_] + ([war_mb[ob % 2]] if war_mb[ob % 2] else []),
                inc="sV")
            war_mb[ob % 2] = E.dma_op(
                "scalar", mish_dr[128 * ob: 128 * (ob + 1), :],
                mbuf[ob % 2][:], waits=[t8_], stream=f"ms{ob % 2}")
            tok_mish[ob] = war_mb[ob % 2]
            if _P3_GHOST:
                drain_ghost(8)
            if "mish" in taps:
                E.dma_op("scalar", taps["mish"][128 * ob: 128 * (ob + 1), :],
                         mbuf[ob % 2][:], waits=[t8_], stream="tap")
    s2a_last = E.fence()
    s2a.close()

    if max_phase <= 2:
        return _finish()

    s2b = contextlib.ExitStack()

    def sb2(name, shape, dt):
        return s2b.enter_context(nc.sbuf_tensor(name, shape, dt))

    tblC = sb2("tblC", [128, 2048], F32)
    tblS = sb2("tblS", [128, 2048], F32)
    p128 = sb2("p128", [128, 128], BF16)
    masks = sb2("masks", [128, 4 * 512], BF16)
    gmj = sb2("gmj", [128, 512], F32)
    wq_sb = sb2("wq_sb", [128, HC * 256], BF16)
    wk_sb = sb2("wk_sb", [128, HC * 256], BF16)
    wv_sb = sb2("wv_sb", [128, HC * 256], BF16)
    q_t = sb2("q_t", [128, 2 * 2048], BF16)
    k_t = sb2("k_t", [128, 2 * 2048], BF16)
    v_t = sb2("v_t", [128, HC * 256], BF16)
    qab = sb2("qab", [128, 2048], BF16)
    t1h = sb2("t1h", [128, 1024], F32)
    t2h = sb2("t2h", [128, 1024], F32)
    s2t = sb2("s2t", [128, 4 * 512], BF16)
    ret_sb = sb2("ret_sb", [128, 2048], F32)
    gam_sb = sb2("gam_sb", [128, 2048], F32)
    stat8 = sb2("stat8", [128, 8], F32)
    st2 = sb2("st2", [128, 2], F32)
    hs22 = sb2("hs22", [2, 2], F32)
    m2 = sb2("m2", [2, 1], F32)
    v2s = sb2("v2s", [2, 1], F32)
    sA2 = sb2("sA2", [2, 1], F32)
    sB2 = sb2("sB2", [2, 1], F32)
    sacol = sb2("sacol", [128, 1], F32)
    bcol = sb2("bcol", [128, 1], F32)
    yc = [sb2(f"yc{i}", [128, 512], F32) for i in range(2)]
    mzb = [sb2(f"mzb{i}", [128, 512], BF16) for i in range(2)]
    zcb = [sb2(f"zcb{i}", [128, 512], BF16) for i in range(2)]

    g2 = s2a_last
    E.dma_op("sync", p128[:], inl["P128"][:], waits=g2, stream="b2")
    E.dma_op("sync", masks[:], inl["MASKS"][:], waits=g2, stream="b2")
    TOK_GMJ = E.dma_op("sync", gmj[:], inl["GMJ"][:], waits=g2, stream="b2")
    TOK_P128 = TOK_MASK = TOK_GMJ
    tok_tbl = [E.dma_op("sync", tblC[:], inl["CK2"][:], waits=g2,
                        stream="tbC"),
               E.dma_op("sync", tblS[:], inl["SK2"][:], waits=g2,
                        stream="tbS")]

    war_w = {"q": [], "k": [], "v": []}
    tok_w = {}
    WSB = {"q": wq_sb, "k": wk_sb, "v": wv_sb}
    WDR = {"q": wq, "k": wk, "v": wv}

    def load_whg(mat, hg):
        tok_w[mat] = E.dma_op(
            "gpsimd", WSB[mat][:, :].rearrange("p (c n) -> p c n", n=256),
            WDR[mat].ap()[:, 256 * hg: 256 * (hg + 1)].rearrange(
                "(c p) n -> p c n", p=128),
            waits=war_w[mat] + g2, stream="w" + mat)

    load_whg("k", 0)
    load_whg("q", 0)
    load_whg("v", 0)

    war_qt = []
    war_kt = []
    war_vt = []
    war_qab = []
    war_s2 = [None] * 4
    war_ret = [None, None, None, None]
    war_gamsb = None
    war_gpair = None
    war_yc = [None, None]
    war_mz = [None, None]
    war_zc = [None, None]
    tok_zrow = {}
    war_stats = []

    def emit_qkv_block(mat, blk, dst, war_dst, tbl_toks, rot=True):
        slot_regs = [0, 1, 2, 3] if blk == 0 else [4, 5, 6, 7]
        psrc = pa if blk == 0 else pb
        wfirst = B.write_waits(slot_regs) + [tok_w[mat]]
        tmm = None
        for hc in range(HC):
            for tq in range(4):
                tmm = E.op("tensor", lambda e, hc=hc, tq=tq, blk=blk,
                           regs=slot_regs, wsb=WSB[mat]: e.matmul(
                               B.ap(regs[tq]),
                               wsb[:, 256 * hc + 128 * blk:
                                   256 * hc + 128 * (blk + 1)],
                               xn1[:, 2048 * hc + 512 * tq:
                                   2048 * hc + 512 * (tq + 1)],
                               start=(hc == 0), stop=(hc == HC - 1),
                               skip_group_check=True),
                           waits=((wfirst if hc == 0 else [])
                                  + [tok_xn1[hc]]), inc="sPE")
        B.set_written(slot_regs, tmm)
        if not rot:
            tevc = E.op("scalar", lambda e, p=psrc, dst=dst, blk=blk:
                        e.activation(dst[:, 2048 * blk: 2048 * (blk + 1)],
                                     p[:], AF.Copy),
                        waits=[tmm] + war_dst + g2, inc="sSc")
            B.set_read(slot_regs, tevc)
            return [tevc, tevc], tmm
        tc0 = E.op("scalar", lambda e, p=psrc: e.copy(qab[:], p[:]),
                   waits=[tmm] + war_qab + g2, inc="sSc")
        B.set_read(slot_regs, tc0)
        out_toks = []
        for th in range(2):
            ta = E.op("vector", lambda e, p=psrc, th=th: e.tensor_mul(
                t1h[:], p[:, 1024 * th: 1024 * (th + 1)],
                tblC[:, 1024 * th: 1024 * (th + 1)]),
                waits=[tmm, tc0] + tbl_toks + g2, inc="sV")
            tqb = None
            if p3_perm:
                for tq in (2 * th, 2 * th + 1):
                    tqb = E.op("tensor", lambda e, tq=tq, regs=slot_regs:
                               e.matmul(
                                   B.ap(regs[tq]), p128[:],
                                   qab[:, 512 * tq: 512 * (tq + 1)],
                                   start=True, stop=True,
                                   skip_group_check=True),
                               waits=[tc0, ta, TOK_P128], inc="sPE")
                war_qab.append(tqb)
                B.set_written([slot_regs[2 * th],
                               slot_regs[2 * th + 1]], tqb)
            tb_ = E.op("vector", lambda e, p=psrc, th=th: e.tensor_mul(
                t2h[:], p[:, 1024 * th: 1024 * (th + 1)],
                tblS[:, 1024 * th: 1024 * (th + 1)]),
                waits=[tqb] + tbl_toks, inc="sV")
            B.set_read([slot_regs[2 * th], slot_regs[2 * th + 1]], tb_)
            to = E.op("vector", lambda e, th=th, dst=dst, blk=blk:
                      e.tensor_add(
                          dst[:, 2048 * blk + 1024 * th:
                              2048 * blk + 1024 * (th + 1)],
                          t1h[:], t2h[:]),
                      waits=[tb_] + war_dst, inc="sV")
            out_toks.append(to)
        return out_toks, tmm

    for hg in range(p3_nhg):
        kt_toks = []
        kmm_last = None
        for blk in (range(2) if p3_kmm else range(0)):
            ts_, kmm_last = emit_qkv_block("k", blk, k_t, war_kt, tok_tbl,
                                           rot=p3_rot)
            kt_toks += ts_
        war_kt = []
        war_w["k"] = [kmm_last] if kmm_last else []
        if not kt_toks:
            kt_toks = [None]
        if p3_ghost:
            drain_ghost(6)
        if hg < p3_nhg - 1:
            load_whg("k", hg + 1)
        tok_tbl = [E.dma_op("sync", tblC[:], inl["CS2"][:], waits=kt_toks,
                            stream="tbC"),
                   E.dma_op("sync", tblS[:], inl["SS2"][:], waits=kt_toks,
                            stream="tbS")]
        qt_toks = []
        qmm_last = None
        for blk in (range(2) if p3_do_q else range(0)):
            ts_, qmm_last = emit_qkv_block("q", blk, q_t, war_qt, tok_tbl,
                                           rot=p3_rot)
            qt_toks += ts_
        if not qt_toks:
            qt_toks = kt_toks
        war_qt = []
        war_w["q"] = [qmm_last] if qmm_last else []
        if p3_ghost:
            drain_ghost(6)
        if hg < p3_nhg - 1:
            load_whg("q", hg + 1)
        vt_toks = []
        vmm_last = None
        for tb_i in (range(HC) if p3_do_v else range(0)):
            reg = tb_i % 2
            w0 = B.write_waits([reg]) + [tok_w["v"]]
            tmm = None
            for hc in range(HC):
                tmm = E.op("tensor", lambda e, hc=hc, tb_i=tb_i, reg=reg:
                           e.matmul(
                               B.ap(reg, 0, 128, 0, 256),
                               xn1[:, 2048 * hc + 128 * tb_i:
                                   2048 * hc + 128 * (tb_i + 1)],
                               wv_sb[:, 256 * hc: 256 * (hc + 1)],
                               start=(hc == 0), stop=(hc == HC - 1),
                               skip_group_check=True),
                           waits=((w0 if hc == 0 else [])
                                  + [tok_xn1[hc]]), inc="sPE")
            vmm_last = tmm
            B.set_written([reg], tmm)
            tv = E.op("scalar", lambda e, tb_i=tb_i, reg=reg: e.activation(
                v_t[:, 256 * tb_i: 256 * (tb_i + 1)],
                B.ap(reg, 0, 128, 0, 256), AF.Copy),
                waits=[tmm] + war_vt, inc="sSc")
            B.set_read([reg], tv)
            vt_toks.append(tv)
            if p3_ghost:
                drain_ghost(1)
        war_vt = []
        war_w["v"] = [vmm_last] if vmm_last else []
        if not vt_toks:
            vt_toks = kt_toks
        if hg < p3_nhg - 1:
            load_whg("v", hg + 1)
            tok_tbl = [E.dma_op("sync", tblC[:], inl["CK2"][:],
                                waits=qt_toks, stream="tbC"),
                       E.dma_op("sync", tblS[:], inl["SK2"][:],
                                waits=qt_toks, stream="tbS")]

        last_qk = None
        last_sv = None
        for p01 in (range(2) if p3_attn else range(0)):
            P = 2 * hg + p01
            g_src = inl["GAM2"].ap()
            tga = E.dma_op(
                "sync", gam_sb[0:64, :],
                g_src[P, 0].unsqueeze(0).partition_broadcast(64)[:, 0],
                waits=[war_gamsb] if war_gamsb else [], stream="gp")
            tgb = E.dma_op(
                "sync", gam_sb[64:128, :],
                g_src[P, 1].unsqueeze(0).partition_broadcast(64)[:, 0],
                waits=[], stream="gp")
            gb_last = tgb
            tok_gam = gb_last
            tok_mz_pre = {}
            for sb_ in range(2):
                tok_mz_pre[sb_] = E.dma_op(
                    "sync", mzb[sb_][:],
                    mish_dr[128 * P: 128 * (P + 1),
                            512 * sb_: 512 * (sb_ + 1)],
                    waits=[tok_mish[P], war_mz[sb_]] if war_mz[sb_]
                    else [tok_mish[P]], stream=f"mz{sb_}")

            ret_toks = []
            sq_toks = []

            def finalize_ret(I, rregs, tsv_h):
                for hh in range(2):
                    B.set_written([rregs[hh]], tsv_h[hh])
                    tev = E.op("vector", lambda e, rreg=rregs[hh], I=I,
                               hh=hh: e.scalar_tensor_tensor(
                                   ret_sb[64 * hh: 64 * (hh + 1),
                                          512 * I: 512 * (I + 1)],
                                   B.ap(rreg, 64 * hh, 64 * (hh + 1)), 1.0,
                                   gam_sb[64 * hh: 64 * (hh + 1),
                                          512 * I: 512 * (I + 1)],
                                   ALU.bypass, ALU.mult,
                                   accum_out=stat8[64 * hh: 64 * (hh + 1),
                                                   I: I + 1]),
                               waits=[tsv_h[hh], tok_gam], inc="sV")
                    tsq = E.op("scalar", lambda e, I=I, hh=hh: e.activation(
                        qab[0:64, 0:512],
                        ret_sb[64 * hh: 64 * (hh + 1),
                               512 * I: 512 * (I + 1)], AF.Square,
                        accum_out=stat8[64 * hh: 64 * (hh + 1),
                                        4 + I: 5 + I]),
                        waits=[tev], inc="sSc")
                    B.set_read([rregs[hh]], tev)
                    war_ret[2 * hh + I % 2] = tev
                    ret_toks.append(tev)
                    sq_toks.append(tsq)

            sv_state = {}

            def emit_sv(st):
                nonlocal last_sv
                I, J, te_h, rregs, first, last = st
                tsv_h = sv_state.setdefault(I, [None, None])
                for hh in range(2):
                    slot = (2 * J + hh) % 4
                    lh = 2 * p01 + hh
                    w_ret = (B.write_waits([rregs[hh]])
                             + [war_ret[I % 2], war_ret[2 + I % 2]]
                             ) if first else []
                    tsv = E.op("tensor", lambda e, hh=hh, J=J, slot=slot,
                               lh=lh, rreg=rregs[hh], first=first,
                               last=last: e.matmul(
                                   B.ap(rreg, 64 * hh, 64 * (hh + 1)),
                                   v_t[:, 256 * J + 64 * lh:
                                       256 * J + 64 * (lh + 1)],
                                   s2t[:, 512 * slot: 512 * (slot + 1)],
                                   start=first, stop=last,
                                   tile_position=(0, 64 * hh),
                                   skip_group_check=True),
                               waits=(w_ret + [te_h[hh], vt_toks[-1]]),
                               inc="sPE")
                    last_sv = tsv
                    tsv_h[hh] = tsv
                    war_s2[slot] = tsv
                if last:
                    finalize_ret(I, rregs, tsv_h)

            steps = [(I, J) for I in range(4) for J in range(4 * I + 4)]
            pend = None
            for (I, J) in steps:
                rregs = [4 + (I % 2), 6 + (I % 2)]
                nJ = 4 * I + 4
                te_h = [None, None]
                tqk_h = [None, None]
                for hh in range(2):
                    sr = (2 * J + hh) % 4
                    wS = B.write_waits([sr])
                    tqk = E.op("tensor", lambda e, hh=hh, sr=sr, J=J,
                               I=I, p01=p01: e.matmul(
                                   B.ap(sr),
                                   k_t[64 * hh: 64 * (hh + 1),
                                       2048 * p01 + 128 * J:
                                       2048 * p01 + 128 * (J + 1)],
                                   q_t[64 * hh: 64 * (hh + 1),
                                       2048 * p01 + 512 * I:
                                       2048 * p01 + 512 * (I + 1)],
                                   start=True, stop=True,
                                   tile_position=(64 * hh, 0),
                                   skip_group_check=True),
                               waits=wS + [kt_toks[-1], qt_toks[-1]],
                               inc="sPE")
                    last_qk = tqk
                    tqk_h[hh] = tqk
                    B.set_written([sr], tqk)
                for hh in range(2):
                    sr = (2 * J + hh) % 4
                    slot = (2 * J + hh) % 4
                    h_g = 2 * P + hh
                    gcol = gmj[:, 32 * J + h_g: 32 * J + h_g + 1]
                    if I == J // 4:
                        te = E.op("vector", lambda e, sr=sr, slot=slot,
                                  gcol=gcol, r=J % 4:
                                  e.scalar_tensor_tensor(
                                      s2t[:, 512 * slot:
                                          512 * (slot + 1)],
                                      B.ap(sr), gcol,
                                      masks[:, 512 * r: 512 * (r + 1)],
                                      ALU.mult, ALU.mult),
                                  waits=[tqk_h[hh], TOK_GMJ, TOK_MASK,
                                         war_s2[slot]] + g2, inc="sV")
                    elif hh == 0:
                        te = E.op("scalar", lambda e, sr=sr, slot=slot,
                                  gcol=gcol: e.activation(
                                      s2t[:, 512 * slot:
                                          512 * (slot + 1)],
                                      B.ap(sr), AF.Copy, scale=gcol),
                                  waits=[tqk_h[hh], TOK_GMJ,
                                         war_s2[slot]] + g2, inc="sSc")
                    else:
                        te = E.op("vector", lambda e, sr=sr, slot=slot,
                                  gcol=gcol: e.tensor_scalar_mul(
                                      s2t[:, 512 * slot:
                                          512 * (slot + 1)],
                                      B.ap(sr), gcol),
                                  waits=[tqk_h[hh], TOK_GMJ,
                                         war_s2[slot]] + g2, inc="sV")
                    B.set_read([sr], te)
                    te_h[hh] = te
                if pend is not None:
                    emit_sv(pend)
                pend = (I, J, te_h, rregs, J == 0, J == nJ - 1)
            emit_sv(pend)
            war_gamsb = ret_toks[-1]
            ts1 = E.op("vector", lambda e: e.tensor_reduce(
                st2[:, 0:1], stat8[:, 0:4], AX.X, ALU.add),
                waits=ret_toks + sq_toks + war_stats, inc="sV")
            ts2 = E.op("vector", lambda e: e.tensor_reduce(
                st2[:, 1:2], stat8[:, 4:8], AX.X, ALU.add),
                waits=sq_toks, inc="sV")
            sreg = 0
            w0 = B.write_waits([sreg])
            tm2 = E.op("tensor", lambda e, sreg=sreg: e.matmul(
                B.ap(sreg, 0, 2, 0, 2), blk2[:], st2[:],
                start=True, stop=True, skip_group_check=True),
                waits=w0 + [ts1, ts2, TOK_BLK2], inc="sPE")
            B.set_written([sreg], tm2)
            th1 = E.op("vector", lambda e, sreg=sreg: e.tensor_copy(
                hs22[:], B.ap(sreg, 0, 2, 0, 2)), waits=[tm2], inc="sV")
            B.set_read([sreg], th1)
            inv = 1.0 / float(S * HD)
            E.op("vector", lambda e, s=inv: e.tensor_scalar_mul(
                m2[:], hs22[:, 0:1], s), waits=[th1], inc="sV")
            E.op("vector", lambda e, s=inv: e.tensor_scalar_mul(
                v2s[:], hs22[:, 1:2], s), waits=[th1], inc="sV")
            E.op("vector", lambda e: e.tensor_mul(sA2[:], m2[:], m2[:]),
                 inc="sV")
            tvr = E.op("vector", lambda e: e.tensor_sub(
                v2s[:], v2s[:], sA2[:]), inc="sV")
            tr1 = E.op("scalar", lambda e: e.activation(
                sB2[:], v2s[:], AF.Ln, bias=float(EPS)),
                waits=[tvr, TOK_C4], inc="sSc")
            tr2 = E.op("scalar", lambda e: e.activation(
                sA2[:], sB2[:], AF.Exp, scale=-0.5), waits=[tr1, TOK_C4],
                inc="sSc")
            tr3 = E.op("vector", lambda e, P=P: e.tensor_mul(
                sA2[:], sA2[:], gnw2[:, P: P + 1]),
                waits=[tr2, TOK_GNW], inc="sV")
            tr4 = E.op("vector", lambda e: e.tensor_mul(
                sB2[:], m2[:], sA2[:]), waits=[tr3], inc="sV")
            tr5 = E.op("vector", lambda e: e.tensor_scalar_mul(
                sB2[:], sB2[:], -1.0), waits=[tr4], inc="sV")
            tr6 = E.op("vector", lambda e, P=P: e.tensor_add(
                sB2[:], sB2[:], gnb2[:, P: P + 1]),
                waits=[tr5, TOK_GNB], inc="sV")
            w0 = B.write_waits([sreg])
            tmA = E.op("tensor", lambda e, sreg=sreg: e.matmul(
                B.ap(sreg, 0, 128, 0, 1), bc2[:], sA2[:],
                start=True, stop=True, skip_group_check=True),
                waits=w0 + [tr3], inc="sPE")
            tmB = E.op("tensor", lambda e, sreg=sreg: e.matmul(
                B.ap(sreg, 0, 128, 1, 2), bc2[:], sB2[:],
                start=True, stop=True, skip_group_check=True),
                waits=[tr6], inc="sPE")
            B.set_written([sreg], tmB)
            tcA = E.op("vector", lambda e, sreg=sreg: e.tensor_copy(
                sacol[:], B.ap(sreg, 0, 128, 0, 1)), waits=[tmB], inc="sV")
            tcB = E.op("vector", lambda e, sreg=sreg: e.tensor_copy(
                bcol[:], B.ap(sreg, 0, 128, 1, 2)), waits=[tmB], inc="sV")
            B.set_read([sreg], tcA)
            B.set_read([sreg], tcB)
            war_stats = [tm2, tmA, tmB, tr1, tr2]
            for sub in range(4):
                ybi = sub % 2
                if sub < 2:
                    tmz = tok_mz_pre[sub]
                else:
                    tmz = E.dma_op(
                        "sync", mzb[ybi][:],
                        mish_dr[128 * P: 128 * (P + 1),
                                512 * sub: 512 * (sub + 1)],
                        waits=[tok_mish[P], war_mz[ybi]], stream=f"mz{ybi}")
                ty = E.op("vector", lambda e, sub=sub, ybi=ybi:
                          e.tensor_scalar(
                              yc[ybi][:],
                              ret_sb[:, 512 * sub: 512 * (sub + 1)],
                              sacol[:], bcol[:], ALU.mult, ALU.add),
                          waits=[tcA, tcB, war_yc[ybi]], inc="sV")
                tz = E.op("vector", lambda e, ybi=ybi: e.tensor_mul(
                    zcb[ybi][:], yc[ybi][:], mzb[ybi][:]),
                    waits=[ty, tmz] + ([war_zc[ybi]] if war_zc[ybi]
                                       else []), inc="sV")
                war_yc[ybi] = tz
                war_mz[ybi] = tz
                war_zc[ybi] = E.dma_op(
                    "scalar", z_dr[128 * P: 128 * (P + 1),
                                   512 * sub: 512 * (sub + 1)],
                    zcb[ybi][:], waits=[tz], stream=f"ys{ybi}")
            tok_zrow[P] = [war_zc[0], war_zc[1]]
            if "y" in taps:
                for sub in range(4):
                    ybi = sub % 2
                    tyy = E.op("vector", lambda e, sub=sub, ybi=ybi:
                               e.tensor_scalar(
                                   yc[ybi][:],
                                   ret_sb[:, 512 * sub: 512 * (sub + 1)],
                                   sacol[:], bcol[:], ALU.mult, ALU.add),
                               waits=[tcA, tcB, war_yc[ybi]], inc="sV")
                    war_yc[ybi] = E.dma_op(
                        "scalar",
                        taps["y"][128 * P: 128 * (P + 1),
                                  512 * sub: 512 * (sub + 1)],
                        yc[ybi][:], waits=[tyy], stream="tap")
        if p3_attn:
            war_kt = [last_qk]
            war_qt = [last_qk]
            war_vt = [last_sv]
    if p3_ghost:
        drain_until(n_gu_steps)
    cc_toks = [None, None, None]
    if p3_cc:
        cc_toks = []
        gst_tok = ("D_gs", E.cnt.get("D_gs", 0))
        for loc, ag in ((locg, agg), (locu, agu)):
            t = E.op("gpsimd", lambda e, loc=loc, ag=ag: e.collective_compute(
                "AllGather", ALU.bypass,
                replica_groups=[list(range(NCORE))],
                ins=[loc[:]], outs=[ag[:]]),
                waits=[gst_tok], inc="cc")
            cc_toks.append(t)
        cc_toks.append(None)
    if "wgate" in taps:
        E.dma_op("scalar", taps["wgate"][:], agg[:], waits=[cc_toks[0]],
                 stream="tap")

    g3 = E.fence()
    E.flush()
    s2b.close()
    s1.close()

    if max_phase <= 3:
        return _finish()

    s3 = contextlib.ExitStack()
    z_t = s3.enter_context(nc.sbuf_tensor("z_t", [128, HC * 2048], BF16))
    wobuf = [s3.enter_context(nc.sbuf_tensor(f"wob{i}", [128, HC * 256],
                                             BF16)) for i in range(2)]
    aobuf = [s3.enter_context(nc.sbuf_tensor(f"aob{i}", [128, 2048], F32))
             for i in range(2)]

    tok_z = {}
    for fc in range(HC):
        bi = fc % 4
        prev = ([tok_z[fc - 4]] if fc - 4 in tok_z else []) + g3
        tok_z[fc] = E.dma_op(
            "sync" if bi < 2 else "scalar",
            z_t[:, 2048 * fc: 2048 * (fc + 1)],
            z_dr[128 * fc: 128 * (fc + 1), :],
            waits=prev + tok_zrow[fc], stream=f"zl{bi}")

    war_wo = [None, None]
    war_ao = [None, None]
    tok_aow = {}
    for q in range(8):
        qi = q % 2
        tok_wo = E.dma_op(
            "gpsimd", wobuf[qi][:, :].rearrange("p (c n) -> p c n", n=256),
            wo.ap()[:, 256 * q: 256 * (q + 1)].rearrange(
                "(c p) n -> p c n", p=128),
            waits=([war_wo[qi]] if war_wo[qi] else []) + g3,
            stream=f"wo{qi}")
        for obl in range(2):
            ob = 2 * q + obl
            slot = ob % 2
            regs = [0, 1, 2, 3] if slot == 0 else [4, 5, 6, 7]
            wfirst = B.write_waits(regs) + [tok_wo]
            tmm = None
            for fc in range(HC):
                for tq in range(4):
                    tmm = E.op("tensor", lambda e, fc=fc, tq=tq, qi=qi,
                               obl=obl, regs=regs: e.matmul(
                                   B.ap(regs[tq]),
                                   wobuf[qi][:, 256 * fc + 128 * obl:
                                             256 * fc + 128 * (obl + 1)],
                                   z_t[:, 2048 * fc + 512 * tq:
                                       2048 * fc + 512 * (tq + 1)],
                                   start=(fc == 0), stop=(fc == HC - 1),
                                   skip_group_check=True),
                               waits=((wfirst if fc == 0 else [])
                                      + [tok_z[fc]]), inc="sPE")
            B.set_written(regs, tmm)
            war_wo[qi] = tmm
            psrc = pa if slot == 0 else pb
            tev = E.op("scalar", lambda e, p=psrc, bi=ob % 2: e.copy(
                aobuf[bi][:], p[:]),
                waits=[tmm] + ([war_ao[ob % 2]] if war_ao[ob % 2] else []),
                inc="sSc")
            B.set_read(regs, tev)
            war_ao[ob % 2] = E.dma_op(
                "scalar", ao_dr[128 * ob: 128 * (ob + 1), :],
                aobuf[ob % 2][:], waits=[tev], stream=f"as{ob % 2}")
            tok_aow[ob] = war_ao[ob % 2]
            if p3_ghost:
                drain_ghost(8)
            if "ao" in taps:
                E.dma_op("scalar", taps["ao"][128 * ob: 128 * (ob + 1), :],
                         aobuf[ob % 2][:], waits=[tev], stream="tap")

    g4 = E.fence()
    E.flush()
    s3.close()

    if max_phase <= 4:
        return _finish()

    s4 = contextlib.ExitStack()
    xb4 = [s4.enter_context(nc.sbuf_tensor(f"xb4_{i}", [128, 2048], F32))
           for i in range(2)]
    xf4 = [s4.enter_context(nc.sbuf_tensor(f"xf4_{i}", [128, 2048], BF16))
           for i in range(2)]
    sq4 = [s4.enter_context(nc.sbuf_tensor(f"sq4_{i}", [128, 2048], BF16))
           for i in range(2)]
    ub4 = s4.enter_context(nc.sbuf_tensor("ub4", [128, 2048], F32))
    ub4b = s4.enter_context(nc.sbuf_tensor("ub4b", [128, 2048], F32))
    yb4 = [s4.enter_context(nc.sbuf_tensor(f"yb4_{i}", [128, 2048], F32))
           for i in range(2)]
    ob4 = [s4.enter_context(nc.sbuf_tensor(f"ob4_{i}", [128, 2048], F32))
           for i in range(2)]

    war_x = [g4, g4]
    tA2 = ln_stats(ao_dr, rA[2], rB[2], (xb4, xf4, sq4), war_x=war_x,
                   spfx="s4_", chunk_toks=tok_aow)
    bcast_rows(rA[2], rB[2], tA2, (0, 1, 2, 3))
    tok_x2w = {}
    war_yb = [[], []]
    war_ob = [None, None]
    for fc in range(HC):
        bi = fc % 2
        tl = E.dma_op("sync", xb4[bi][:],
                      ao_dr[128 * fc: 128 * (fc + 1), :], waits=war_x[bi],
                      stream=f"s4_{bi}")
        tl2 = E.dma_op("sync", yb4[bi][:],
                       xT[128 * fc: 128 * (fc + 1), :],
                       waits=war_yb[bi] + g4, stream=f"rt{bi}")
        tu = E.op("vector", lambda e, b=bi: e.tensor_mul(
            ub4[:], xb4[b][:], pa[:]),
            waits=[tl] + B.read_waits([0, 1, 2, 3]), inc="sV")
        war_x[bi] = [tu]
        tu2 = E.op("vector", lambda e: e.tensor_add(ub4b[:], ub4[:], pb[:]),
                   waits=[tu] + B.read_waits([4, 5, 6, 7]), inc="sV")
        tx2 = E.op("vector", lambda e, b=bi: e.tensor_add(
            ob4[b][:], ub4b[:], yb4[b][:]),
            waits=[tu2, tl2] + ([war_ob[bi]] if war_ob[bi] else []),
            inc="sV")
        war_yb[bi] = [tx2]
        B.set_read([0, 1, 2, 3], tu)
        B.set_read([4, 5, 6, 7], tu2)
        tok_x2w[fc] = E.dma_op("scalar", x2_dr[128 * fc: 128 * (fc + 1), :],
                               ob4[bi][:], waits=[tx2], stream=f"x2s{bi}")
        war_ob[bi] = tok_x2w[fc]
        if p3_ghost:
            drain_ghost(5)
        if "x2" in taps:
            E.dma_op("scalar", taps["x2"][128 * fc: 128 * (fc + 1), :],
                     ob4[bi][:], waits=[tx2], stream="tap")
    if p3_ghost:
        drain_ghost(len(ghost_steps))
    if p3_cc:
        gst2_tok = ("D_gs", E.cnt.get("D_gs", 0))
        cc_toks[2] = E.op("gpsimd", lambda e: e.collective_compute(
            "AllGather", ALU.bypass,
            replica_groups=[list(range(NCORE))],
            ins=[locd[:]], outs=[agd[:]]),
            waits=[gst2_tok], inc="cc")
    war_x = [[tok_x2w[HC - 2]], [tok_x2w[HC - 1]]]
    tA3 = ln_stats(x2_dr, rA[3], rB[3], (xb4, xf4, sq4), war_x=war_x,
                   spfx="s4_")

    g4b = E.fence()
    E.flush()
    s4.close()
    sGH.close()

    if max_phase <= 5:
        return _finish()

    s5 = contextlib.ExitStack()
    hmid = s5.enter_context(nc.sbuf_tensor("hmid", [128, NIB * 1024], BF16))
    silu_s = s5.enter_context(nc.sbuf_tensor("silu_s", [128, 1024], F32))
    x2b = [s5.enter_context(nc.sbuf_tensor(f"x2b{i}", [128, 1024], F32))
           for i in range(2)]
    fxb = [s5.enter_context(nc.sbuf_tensor(f"fxb{i}", [128, 1024], F32))
           for i in range(2)]
    u5a = s5.enter_context(nc.sbuf_tensor("u5a", [128, 1024], F32))
    u5b = s5.enter_context(nc.sbuf_tensor("u5b", [128, 1024], F32))
    xb5 = [s5.enter_context(nc.sbuf_tensor(f"xb5_{i}", [128, 1024], F32))
           for i in range(2)]
    fbuf = xb5
    xf5 = [s5.enter_context(nc.sbuf_tensor(f"xf5_{i}", [128, 1024], BF16))
           for i in range(2)]
    sq5 = [s5.enter_context(nc.sbuf_tensor(f"sq5_{i}", [128, 1024], BF16))
           for i in range(2)]

    g5 = g4b + E.fence()
    for tc in range(2):
        stc = contextlib.ExitStack()
        xn2 = stc.enter_context(
            nc.sbuf_tensor(f"xn2_{tc}", [128, HC * 1024], BF16))
        gbw = [stc.enter_context(nc.sbuf_tensor(
            f"gbw{tc}_{i}", [128, HC * 128], BF16)) for i in range(2)]
        ubw = [stc.enter_context(nc.sbuf_tensor(
            f"ubw{tc}_{i}", [128, HC * 128], BF16)) for i in range(2)]
        bcast_rows(rA[3], rB[3], tA3, (2 * tc, 2 * tc + 1))
        war_x2b = [[], []]
        tok_xn2 = {}
        for hc in range(HC):
            bi = hc % 2
            tl = E.dma_op("sync", x2b[bi][:],
                          x2_dr[128 * hc: 128 * (hc + 1),
                                1024 * tc: 1024 * (tc + 1)],
                          waits=war_x2b[bi] + [tok_x2w[hc]] + g5,
                          stream=f"x5{bi}")
            tu = E.op("vector", lambda e, b=bi: e.tensor_mul(
                u5a[:], x2b[b][:], pa[:, 0:1024]),
                waits=[tl] + B.read_waits([0, 1]) + g5, inc="sV")
            war_x2b[bi] = [tu]
            tok_xn2[hc] = E.op("vector", lambda e, hc=hc: e.tensor_add(
                xn2[:, 1024 * hc: 1024 * (hc + 1)], u5a[:],
                pb[:, 0:1024]), waits=[tu] + B.read_waits([4, 5]), inc="sV")
            B.set_read([0, 1], tu)
            B.set_read([4, 5], tok_xn2[hc])
        war_gb = [None, None]
        war_silu = None
        tok_hm = {}
        for ib in range(NIB):
            qi = ib % 2
            wq_ = [war_gb[qi]] if war_gb[qi] else []
            tg = E.dma_op(
                "sync", gbw[qi][:, :].rearrange("p (c n) -> p c n", n=128),
                agg.ap()[:, 128 * ib: 128 * (ib + 1)].rearrange(
                    "(c p) n -> p c n", p=128),
                waits=wq_ + [cc_toks[0]] + g5, stream=f"gw{qi}")
            tu_ = E.dma_op(
                "sync", ubw[qi][:, :].rearrange("p (c n) -> p c n", n=128),
                agu.ap()[:, 128 * ib: 128 * (ib + 1)].rearrange(
                    "(c p) n -> p c n", p=128),
                waits=wq_ + [cc_toks[1]] + g5, stream=f"uw{qi}")
            slot = ib % 2
            gregs = [0, 1] if slot == 0 else [4, 5]
            uregs = [2, 3] if slot == 0 else [6, 7]
            wfirst = B.write_waits(gregs + uregs) + [tg, tu_]
            tmm = None
            for hc in range(HC):
                for nh in range(2):
                    for (wsb, regs) in ((gbw[qi], gregs), (ubw[qi], uregs)):
                        tmm = E.op("tensor", lambda e, hc=hc, nh=nh,
                                   wsb=wsb, regs=regs: e.matmul(
                                       B.ap(regs[nh]),
                                       wsb[:, 128 * hc: 128 * (hc + 1)],
                                       xn2[:, 1024 * hc + 512 * nh:
                                           1024 * hc + 512 * (nh + 1)],
                                       start=(hc == 0),
                                       stop=(hc == HC - 1),
                                       skip_group_check=True),
                                   waits=((wfirst if hc == 0 else [])
                                          + [tok_xn2[hc]]), inc="sPE")
            B.set_written(gregs + uregs, tmm)
            war_gb[qi] = tmm
            pg = pa if slot == 0 else pb
            tsl = E.op("scalar", lambda e, p=pg: e.activation(
                silu_s[:], p[:, 0:1024], AF.Silu),
                waits=[tmm, war_silu], inc="sSc")
            B.set_read(gregs, tsl)
            thm = E.op("vector", lambda e, p=pg, ib=ib: e.tensor_mul(
                hmid[:, 1024 * ib: 1024 * (ib + 1)], silu_s[:],
                p[:, 1024:2048]), waits=[tsl], inc="sV")
            war_silu = thm
            B.set_read(uregs, thm)
            tok_hm[ib] = thm
        stc_mid = E.fence()
        stc.close()
        std = contextlib.ExitStack()
        dbw = [std.enter_context(nc.sbuf_tensor(
            f"dbw{tc}_{i}", [128, NIB * 256], BF16)) for i in range(2)]
        war_db = [None, None]
        war_fb = [None, None]
        tok_ffnw = {}
        for op2 in range(8):
            di = op2 % 2
            td = E.dma_op(
                "sync", dbw[di][:, :].rearrange("p (c n) -> p c n", n=256),
                agd.ap()[:, 256 * op2: 256 * (op2 + 1)].rearrange(
                    "(c p) n -> p c n", p=128),
                waits=([war_db[di]] if war_db[di] else []) + [cc_toks[2]]
                + stc_mid, stream=f"dw{di}")
            for ol in range(2):
                ob = 2 * op2 + ol
                slot = ob % 2
                regs = [0, 1] if slot == 0 else [4, 5]
                wfirst = B.write_waits(regs) + [td]
                tmm = None
                for ib in range(NIB):
                    for nh in range(2):
                        tmm = E.op("tensor", lambda e, ib=ib, nh=nh, ol=ol,
                                   di=di, regs=regs: e.matmul(
                                       B.ap(regs[nh]),
                                       dbw[di][:, 256 * ib + 128 * ol:
                                               256 * ib + 128 * (ol + 1)],
                                       hmid[:, 1024 * ib + 512 * nh:
                                            1024 * ib + 512 * (nh + 1)],
                                       start=(ib == 0),
                                       stop=(ib == NIB - 1),
                                       skip_group_check=True),
                                   waits=((wfirst if ib == 0 else [])
                                          + [tok_hm[ib]]), inc="sPE")
                B.set_written(regs, tmm)
                war_db[di] = tmm
                pd_ = pa if slot == 0 else pb
                tev = E.op("scalar", lambda e, p=pd_, bi=ob % 2: e.copy(
                    fbuf[bi][:], p[:, 0:1024]),
                    waits=[tmm] + ([war_fb[ob % 2]] if war_fb[ob % 2]
                                   else []), inc="sSc")
                B.set_read(regs, tev)
                war_fb[ob % 2] = E.dma_op(
                    "scalar", ffn_dr[128 * ob: 128 * (ob + 1),
                                     1024 * tc: 1024 * (tc + 1)],
                    fbuf[ob % 2][:], waits=[tev], stream=f"fs{ob % 2}")
                tok_ffnw[ob] = war_fb[ob % 2]
                if "ffn" in taps:
                    E.dma_op("scalar",
                             taps["ffn"][128 * ob: 128 * (ob + 1),
                                         1024 * tc: 1024 * (tc + 1)],
                             fbuf[ob % 2][:], waits=[tev], stream="tap")
        war_x5 = [[tok_ffnw[HC - 2]], [tok_ffnw[HC - 1]]]
        tA4 = ln_stats(ffn_dr, rA[4], rB[4], (xb5, xf5, sq5),
                       tc_list=(2 * tc, 2 * tc + 1), col0=1024 * tc,
                       colw=1024, war_x=war_x5, spfx="s5_",
                       chunk_toks=tok_ffnw)
        bcast_rows(rA[4], rB[4], tA4, (2 * tc, 2 * tc + 1))
        war_fxb = [list(war_x5[0]), list(war_x5[1])]
        war_o5 = [None, None]
        for ob in range(HC):
            bi = ob % 2
            tl = E.dma_op("sync", fxb[bi][:],
                          ffn_dr[128 * ob: 128 * (ob + 1),
                                 1024 * tc: 1024 * (tc + 1)],
                          waits=war_fxb[bi] + [tok_ffnw[ob]],
                          stream=f"fx{bi}")
            tl2 = E.dma_op("sync", x2b[bi][:],
                           x2_dr[128 * ob: 128 * (ob + 1),
                                 1024 * tc: 1024 * (tc + 1)],
                           waits=war_x2b[bi], stream=f"x5{bi}")
            tu = E.op("vector", lambda e, b=bi: e.tensor_mul(
                u5a[:], fxb[b][:], pa[:, 0:1024]),
                waits=[tl] + B.read_waits([0, 1]), inc="sV")
            tu2 = E.op("vector", lambda e: e.tensor_add(
                u5b[:], u5a[:], pb[:, 0:1024]),
                waits=[tu] + B.read_waits([4, 5]), inc="sV")
            tou = E.op("vector", lambda e, b=bi: e.tensor_add(
                fxb[b][:], u5b[:], x2b[b][:]),
                waits=[tu2, tl2], inc="sV")
            war_fxb[bi] = [tou]
            war_x2b[bi] = [tou]
            B.set_read([0, 1], tu)
            B.set_read([4, 5], tu2)
            tst = E.dma_op(
                "scalar", outT[128 * ob: 128 * (ob + 1),
                               1024 * tc: 1024 * (tc + 1)],
                fxb[bi][:], waits=[tou], stream=f"os{bi}")
            war_fxb[bi] = [tst]
            war_o5[bi] = tst
        E.barrier()
        E.flush()
        std.close()
    s5.close()

    return _finish()


def kernel(L0_attn_x, L0_attn_wqkv, L0_attn_wg, L0_attn_wo, L0_attn_gn_weight,
           L0_attn_gn_bias, L0_ffn_lut, L0_ffn_gate_idx, L0_ffn_up_idx,
           L0_ffn_down_idx):
    x = np.asarray(L0_attn_x, np.float32)
    wqkv = np.asarray(L0_attn_wqkv, np.float32)
    wg_ = np.asarray(L0_attn_wg, np.float32)
    wo_ = np.asarray(L0_attn_wo, np.float32)
    gnw_ = np.ascontiguousarray(np.asarray(L0_attn_gn_weight, np.float32))
    gnb_ = np.ascontiguousarray(np.asarray(L0_attn_gn_bias, np.float32))
    lut_ = np.ascontiguousarray(np.asarray(L0_ffn_lut, np.float32))
    gidx = np.asarray(L0_ffn_gate_idx, np.int32)
    uidx = np.asarray(L0_ffn_up_idx, np.int32)
    didx = np.asarray(L0_ffn_down_idx, np.int32)

    w3 = wqkv.reshape(NH, 3 * HD, D)
    wq_h = np.ascontiguousarray(w3[:, 0:HD].reshape(D, D).T)
    wk_h = np.ascontiguousarray(w3[:, HD: 2 * HD].reshape(D, D).T)
    wv_h = np.ascontiguousarray(w3[:, 2 * HD: 3 * HD].reshape(D, D).T)
    wg_h = np.ascontiguousarray(wg_.T)
    wo_h = np.ascontiguousarray(wo_.T)
    gT = np.ascontiguousarray(gidx.T)
    uT = np.ascontiguousarray(uidx.T)
    dT = np.ascontiguousarray(didx.T)

    nc = build_program(_MAX_PHASE)
    in_maps = []
    for c in range(NCORE):
        in_maps.append({
            "xT": np.ascontiguousarray(x[c].T),
            "wq": wq_h, "wk": wk_h, "wv": wv_h, "wg": wg_h, "wo": wo_h,
            "lut": lut_, "gnw": gnw_, "gnb": gnb_,
            "gsl": np.ascontiguousarray(gT[GSL * c: GSL * (c + 1)]),
            "usl": np.ascontiguousarray(uT[GSL * c: GSL * (c + 1)]),
            "dsl": np.ascontiguousarray(dT[DSL * c: DSL * (c + 1)]),
        })
    res = run_bass_kernel_spmd(nc, in_maps, core_ids=list(range(NCORE)))
    out = np.empty((NCORE, S, D), np.float32)
    for c in range(NCORE):
        out[c] = np.asarray(res.results[c]["outT"], np.float32).T
    return out
